# version 15
# baseline (speedup 1.0000x reference)
import math
from contextlib import ExitStack

import numpy as np
import ml_dtypes

import concourse.bass as bass
import concourse.mybir as mybir
from concourse.bass_utils import run_bass_kernel_spmd

F32 = mybir.dt.float32
BF16 = mybir.dt.bfloat16
AF = mybir.ActivationFunctionType
ALU = mybir.AluOpType
AX = mybir.AxisListType

B, S, D = 4, 4096, 4096
H, KV, HD = 32, 8, 128
NSLOT = 16
BOUND = 40.0

ENGS = ("sync", "gpsimd", "scalar", "vector", "tensor")
ENGSEM = {"tensor": "pe", "vector": "dv", "scalar": "ac", "gpsimd": "gp", "sync": None}
QSEM = {"sync": "qL", "gpsimd": "qG", "scalar": "qS"}
SEMS = ("qL", "qG", "qS", "pe", "dv", "ac", "gp")


class Plan:

    def __init__(self):
        self.streams = {e: [] for e in ENGS}
        self.cnt = {}
        self.waited = {e: {} for e in ENGS}
        self.pending = {e: [] for e in ENGS}
        self.sem_owner = {}
        for s in ("pe", "dv", "ac", "gp", "sy"):
            self.cnt[s] = 0

    def E(self, eng, fn, waits=(), inc=False, dma=None, drain=False):
        wl = list(self.pending[eng])
        self.pending[eng] = []
        wl += [w for w in waits if w is not None]
        eff = []
        own = ENGSEM[eng]
        for (s, v) in wl:
            if s == own:
                continue
            if v <= self.waited[eng].get(s, 0):
                continue
            self.waited[eng][s] = v
            eff.append((s, v))
        incs = []
        tok = None
        if dma is not None:
            assert eng in QSEM, eng
            self.sem_owner.setdefault(dma, eng)
            assert self.sem_owner[dma] == eng, (dma, eng)
            self.cnt[dma] = self.cnt.get(dma, 0) + 16
            incs.append((dma, 16))
            tok = (dma, self.cnt[dma])
        elif inc and not drain:
            self.cnt[own] += 1
            incs.append((own, 1))
            tok = (own, self.cnt[own])
        self.streams[eng].append([eff, fn, incs])
        if drain:
            dincs = []
            if inc:
                self.cnt[own] += 1
                dincs.append((own, 1))
                tok = (own, self.cnt[own])
            self.streams[eng].append([[], lambda e: e.drain(), dincs])
        return tok

    def D(self, eng):
        self.streams[eng].append([[], lambda e: e.drain(), []])

    def barrier(self):
        toks = []
        for e in ENGS:
            st = self.streams[e]
            if not st:
                continue
            eff, fn, incs = st[-1]
            own = ENGSEM[e] or "gp"
            if incs:
                toks.append((incs[-1][0], self.cnt[incs[-1][0]]))
            else:
                self.cnt[own] += 1
                incs.append((own, 1))
                toks.append((own, self.cnt[own]))
            if e in QSEM:
                drains = [(s, self.cnt[s]) for s, o in self.sem_owner.items()
                          if o == e and self.cnt[s] > self.waited[e].get(s, 0)]
                if drains:
                    self.E(e, lambda en: en.nop(), waits=drains)
                    own2 = {"sync": "sy", "gpsimd": "gp", "scalar": "ac"}[e]
                    self.cnt[own2] = self.cnt.get(own2, 0) + 1
                    self.streams[e][-1][2].append((own2, 1))
                    toks.append((own2, self.cnt[own2]))
        for e2 in ENGS:
            self.pending[e2] += toks


def ceil_div(a, b):
    return -(-a // b)


def build_program(dbg=False, stop_after="s8"):
    nc = bass.Bass("TRN2", target_bir_lowering=False)

    def inp(name, shape, dt=F32):
        return nc.dram_tensor(name, shape, dt, kind="ExternalInput").ap()

    x_own = inp("x_own", [NSLOT, 128, D])
    x_kv = inp("x_kv", [S, D])
    wq_t = inp("wq_t", [D, D])
    wk_t = inp("wk_t", [D, KV * HD])
    wv_t = inp("wv_t", [D, KV * HD])
    wo_t = inp("wo_t", [D, D])
    w_t = inp("w_t", [D, D])
    a_t = inp("a_t", [D, 128])
    bc_f = inp("bc_f", [128, D])
    bias_f = inp("bias_f", [1, D])
    wrep = inp("wrep", [128, 1])
    cosk_t = inp("cosk_t", [128, S])
    sink_t = inp("sink_t", [128, S])
    cosq_t = inp("cosq_t", [128, NSLOT * 128])
    sinq_t = inp("sinq_t", [128, NSLOT * 128])
    maskmin = inp("maskmin", [128, 4 * 256])
    ccol = inp("ccol", [128, 8])
    ident_i = inp("ident", [128, 128], BF16)
    ones_col_i = inp("ones_col", [128, 1], BF16)
    ones_row_f_i = inp("ones_row_f", [1, 128])
    ones_row_bf_i = inp("ones_row_bf", [1, 128], BF16)
    ones_mat_i = inp("ones_mat", [128, 128], BF16)

    out_own = nc.dram_tensor("out_own", [NSLOT, 128, D], F32, kind="ExternalOutput").ap()

    def scr(name, shape, dt):
        kind = "ExternalOutput" if dbg else "Internal"
        return nc.dram_tensor(name, shape, dt, kind=kind).ap()

    xnT_scr = scr("xnT_scr", [8, 128, 32, 512], BF16)
    xnTo_scr = scr("xnTo_scr", [4, 128, 32, 512], BF16)
    K_scr = scr("K_scr", [KV, 128, S], BF16)
    V_scr = scr("V_scr", [KV, 128, 32, 128], BF16)
    Q_scr = scr("Q_scr", [KV, 128, 4, NSLOT, 128], BF16)
    attn_scr = scr("attn_scr", [NSLOT, 128, H, 128], BF16)
    y_scr = scr("y_scr", [NSLOT, 128, D], F32)
    x2_scr = scr("x2_scr", [NSLOT, 128, D], F32)
    xn2T_scr = scr("xn2T_scr", [NSLOT, 128, 32, 128], BF16)
    f_scr = scr("f_scr", [NSLOT, 128, D], F32)
    if dbg and dbg >= 2:
        dbg_sums = scr("dbg_sums", [KV, NSLOT, 4, 128], F32)
        dbg_rcp = scr("dbg_rcp", [KV, NSLOT, 4, 128], F32)
        dbg_pt = scr("dbg_pt", [NSLOT, 128, D], BF16)

    al = nc.alloc_sbuf_tensor
    WBUF = al("WBUF", [128, 32, 1024], BF16)
    TBUF = al("TBUF", [128, 16384], BF16)
    XBUF = al("XBUF", [128, 2, 4096], F32)
    PBUF = al("PBUF", [128, 2, 4096], BF16)
    ABUF = al("ABUF", [128, 8192], BF16)
    STG = al("STG", [128, 2, 1024], BF16)
    RBUF = al("RBUF", [128, 2, 2, 512], F32)
    RTMP = al("RTMP", [128, 2, 512], F32)
    MSK2 = al("MSK2", [128, 4, 256], F32)
    LNS = al("LNS", [128, 2, 8], F32)
    LNP = al("LNP", [128, 8], F32)
    LN2S = al("LN2S", [128, NSLOT, 8], F32)
    LN2Q = al("LN2Q", [128, NSLOT, 8], F32)
    LN4S = al("LN4S", [128, NSLOT, 8], F32)
    LN4Q = al("LN4Q", [128, NSLOT, 8], F32)
    CC = al("CC", [128, 8], F32)
    CO = al("CO", [128, 1], F32)
    WREP = al("WREP", [128, 1], F32)
    IDT = al("IDT", [128, 128], BF16)
    OCOL = al("OCOL", [128, 1], BF16)
    ORF = al("ORF", [1, 128], F32)
    ORB = al("ORB", [1, 128], BF16)
    BIAS = al("BIAS", [1, D], BF16)
    SUMS = al("SUMS", [1, 256], F32)
    RCP = al("RCP", [1, 2, 256], BF16)
    OMAT = al("OMAT", [128, 128], BF16)
    YST = al("YST", [128, 2, 512], F32)
    BCS = al("BCS", [128, 2, 256], F32)

    c_one = CC[:, 0:1]
    c_two = CC[:, 2:3]
    c_eps = CC[:, 3:4]
    c_invdm = CC[:, 4:5]
    c_negb = CC[:, 5:6]
    nc.const_aps.register(F32, 1e-5, c_eps)

    fb = [nc.alloc_psum_tensor(f"fb{i}", [128, 512], F32) for i in range(6)]
    tb_ps = [nc.alloc_psum_tensor(f"tp{i}", [128, 128], BF16) for i in range(2)]

    p = Plan()
    E = p.E

    t_rstd = [None, None]
    t_lnuse = [None, None]

    def ln_stats_fin(sum_ap, ssq_ap, par, waits):
        L = LNS[:, par, :]
        E("scalar", lambda e, o=L[:, 1:2], i=sum_ap: e.activation(
            out=o, in_=i, func=AF.Copy, bias=0.0, scale=c_invdm),
          waits=list(waits) + [t_lnuse[par]])
        E("scalar", lambda e, o=L[:, 0:1], i=ssq_ap: e.activation(
            out=o, in_=i, func=AF.Copy, bias=0.0, scale=c_invdm))
        E("scalar", lambda e, o=L[:, 3:4], i=L[:, 1:2]: e.activation(
            out=o, in_=i, func=AF.Square, bias=0.0, scale=1.0))
        E("scalar", lambda e, o=L[:, 4:5], i=L[:, 3:4]: e.activation(
            out=o, in_=i, func=AF.Copy, bias=1e-5, scale=-1.0))
        E("scalar", lambda e, o=L[:, 6:7], i=L[:, 4:5]: e.activation(
            out=o, in_=i, func=AF.Copy, bias=0.0, scale=1.0))
        E("scalar", lambda e, o=L[:, 5:6], i=L[:, 6:7]: e.add(
            out=o, in_=i, add=L[:, 0:1]))
        E("scalar", lambda e, o=L[:, 6:7], i=L[:, 5:6]: e.activation(
            out=o, in_=i, func=AF.Ln, bias=0.0, scale=1.0))
        tr = E("scalar", lambda e, o=L[:, 7:8], i=L[:, 6:7]: e.activation(
            out=o, in_=i, func=AF.Exp, bias=0.0, scale=-0.5),
            inc=True, drain=True)
        t_rstd[par] = tr
        return L[:, 1:2], L[:, 7:8], tr

    def ln_full(x_ap, xn_ap, scrap_ap, par, waits, scrap_war=None):
        L = LNS[:, par, :]
        tcert = E("vector", lambda e, o=L[:, 0:1], i=x_ap: e.tensor_reduce(
            out=o, in_=i, axis=AX.X, op=ALU.add),
          waits=list(waits) + [t_rstd[par]], inc=True, drain=True)
        E("scalar", lambda e, o=scrap_ap, i=x_ap, a=L[:, 2:3]:
          e.activation(out=o, in_=i, func=AF.Square, bias=0.0, scale=1.0,
                       accum_out=a), waits=list(waits))
        mean, rstd, tr = ln_stats_fin(L[:, 0:1], L[:, 2:3], par, [tcert])
        txn = E("vector", lambda e, o=xn_ap, i=x_ap, m=mean, r=rstd: e.tensor_scalar(
            out=o, in0=i, scalar1=m, scalar2=r, op0=ALU.subtract, op1=ALU.mult),
          waits=[tr], inc=True)
        t_lnuse[par] = txn
        return txn

    tr_war = [None, None]

    def transpose_chunks(src_ap_fn, dst_ap_fn, n, t_src, first_waits=()):
        tpe = tac = None
        for c in range(n):
            par = c % 2
            w = [t_src, tr_war[par]]
            if c == 0:
                w += list(first_waits)
            tpe = E("tensor", lambda e, o=tb_ps[par][:, :], i=src_ap_fn(c): e.transpose(
                o, i, IDT[:, :]), waits=w, inc=True)
            tac = E("scalar", lambda e, o=dst_ap_fn(c), i=tb_ps[par][:, :]: e.copy(
                out=o, in_=i), waits=[tpe], inc=True)
            tr_war[par] = tac
        return tpe, tac

    def rope5(ps_ap, cos_ap, sin_ap, out_ap, waits):
        t1 = RTMP[:, 0, :]
        u = RTMP[:, 1, :]
        E("vector", lambda e, o=t1, a=ps_ap, b=cos_ap: e.tensor_tensor(
            out=o, in0=a, in1=b, op=ALU.mult), waits=waits)
        E("vector", lambda e, o=u[0:64, :], a=ps_ap[64:128, :], b=sin_ap[0:64, :]:
          e.tensor_tensor(out=o, in0=a, in1=b, op=ALU.mult))
        E("vector", lambda e, o=u[64:128, :], a=ps_ap[0:64, :], b=sin_ap[64:128, :]:
          e.tensor_tensor(out=o, in0=a, in1=b, op=ALU.mult))
        E("vector", lambda e, o=out_ap[0:64, :], a=t1[0:64, :], b=u[0:64, :]:
          e.tensor_tensor(out=o, in0=a, in1=b, op=ALU.subtract))
        tk = E("vector", lambda e, o=out_ap[64:128, :], a=t1[64:128, :], b=u[64:128, :]:
               e.tensor_tensor(out=o, in0=a, in1=b, op=ALU.add), inc=True)
        return tk

    def load_weight_cast(dram_ap_3d, dst_ap_3d, nsplit, sem, waits=(),
                         per_split=False):
        C = dram_ap_3d.shape[1]
        step = C // nsplit
        toks = []
        for i in range(nsplit):
            w = list(waits) if i == 0 else ()
            sm = sem[i] if per_split else sem
            toks.append(E("gpsimd",
                          lambda e, o=dst_ap_3d[:, i * step:(i + 1) * step, :],
                          s=dram_ap_3d[:, i * step:(i + 1) * step, :]:
                          e.dma_start(out=o, in_=s), waits=w, dma=sm))
        return toks if per_split else toks[-1]

    def _finish():
        p.barrier()
        E("gpsimd", lambda e: e.nop())
        n_ins = {e: len(p.streams[e]) for e in ENGS}
        print("kernel build: instructions per engine:", n_ins,
              "total", sum(n_ins.values()),
              "sbuf_remaining", nc.sbuf_bytes_remaining,
              "psum_banks_remaining", nc.psum_banks_remaining)
        with ExitStack() as ctx:
            sems = {n: ctx.enter_context(nc.semaphore(n)) for n in p.cnt}
            block = ctx.enter_context(nc.Block())

            def make_replayer(stream):
                def f(eng):
                    for eff, fn, incs in stream:
                        for (s, v) in eff:
                            eng.wait_ge(sems[s], v)
                        ins = fn(eng)
                        for (s, a) in incs:
                            ins.then_inc(sems[s], a)
                return f

            block.sync(make_replayer(p.streams["sync"]))
            block.gpsimd(make_replayer(p.streams["gpsimd"]))
            block.scalar(make_replayer(p.streams["scalar"]))
            block.vector(make_replayer(p.streams["vector"]))
            block.tensor(make_replayer(p.streams["tensor"]))
        return nc

    E("sync", lambda e: e.dma_start(out=CC[:, :], in_=ccol), dma="lpre")
    E("sync", lambda e: e.dma_start(out=IDT[:, :], in_=ident_i), dma="lpre")
    E("sync", lambda e: e.dma_start(out=OCOL[:, :], in_=ones_col_i), dma="lpre")
    E("sync", lambda e: e.dma_start(out=ORF[:, :], in_=ones_row_f_i), dma="lpre")
    E("sync", lambda e: e.dma_start(out=ORB[:, :], in_=ones_row_bf_i), dma="lpre")
    E("sync", lambda e: e.dma_start(out=MSK2[:, :, :], in_=maskmin.rearrange(
        "p (a b) -> p a b", a=4)), dma="lpre")
    E("sync", lambda e: e.dma_start(out=OMAT[:, :], in_=ones_mat_i), dma="lpre")
    tpre = E("sync", lambda e: e.dma_start(out=WREP[:, :], in_=wrep), dma="lpre")
    tcc = tpre
    E("vector", lambda e: e.tensor_tensor(out=CO[:, :], in0=WREP[:, :], in1=c_two,
                                          op=ALU.mult), waits=[tpre])

    wk_v = wk_t.rearrange("(c p) m -> p c m", p=128)
    wv_v = wv_t.rearrange("(c p) m -> p c m", p=128)
    wq_v = wq_t.rearrange("(c p) m -> p c m", p=128)
    wo_v = wo_t.rearrange("(c p) m -> p c m", p=128)
    w_v = w_t.rearrange("(c p) m -> p c m", p=128)
    a_v = a_t.rearrange("(c p) m -> p c m", p=128)

    wb3 = WBUF[:, :, :]
    wb4 = WBUF[:, :, :].rearrange("p c (s m) -> p c s m", s=2)
    tb32 = TBUF[:, :].rearrange("p (c t) -> p c t", c=32)

    tw = load_weight_cast(wk_v, wb3, 4, "gW0")
    x_cons = [None, None]
    pb_cons = [None, None]
    rope_tok = [None, None]
    kstore = [None, None]
    rbuf_war = [None, None]
    scrap_tok = None
    tbuf_store = None

    for st in range(8):
        rp = st % 2
        E("sync", lambda e, o=RBUF[:, rp, 0, :],
          s=cosk_t[:, st * 512:(st + 1) * 512]: e.dma_start(out=o, in_=s),
          waits=[rbuf_war[rp]], dma="lRB%d" % rp)
        rbuf_tok = E("sync", lambda e, o=RBUF[:, rp, 1, :],
                     s=sink_t[:, st * 512:(st + 1) * 512]: e.dma_start(out=o, in_=s),
                     dma="lRB%d" % rp)
        ac_last = None
        for q in range(4):
            ti = st * 4 + q
            sl = q % 2
            tx = E("sync", lambda e, o=XBUF[:, sl, :],
                   s=x_kv[ti * 128:(ti + 1) * 128, :]: e.dma_start(out=o, in_=s),
                   waits=[x_cons[sl], tcc], dma="lXB%d" % sl)
            txn = ln_full(XBUF[:, sl, :], PBUF[:, sl, :], ABUF[:, 0:4096], sl,
                          waits=[tx], scrap_war=scrap_tok)
            scrap_tok = txn
            x_cons[sl] = txn
            tpe, tac = transpose_chunks(
                lambda c, sl=sl: PBUF[:, sl, c * 128:(c + 1) * 128],
                lambda c, q=q: tb32[:, c, q * 128:(q + 1) * 128],
                32, txn, first_waits=[pb_cons[sl], tbuf_store])
            pb_cons[sl] = tpe
            ac_last = tac
        tbuf_store = E("scalar", lambda e, o=xnT_scr[st, :, :, :], s=tb32[:, :, :]:
                       e.dma_start(out=o, in_=s), waits=[ac_last], dma="sTB")
        for c in range(KV):
            bp = c % 2
            tmm = None
            for dc in range(32):
                w = ()
                if dc == 0:
                    w = [ac_last, rope_tok[bp]] + ([tw] if (st == 0 and c == 0) else [])
                tmm = E("tensor", lambda e, o=fb[bp][:, :],
                        l=wb3[:, dc, c * 128:(c + 1) * 128], r=tb32[:, dc, :],
                        s=(dc == 0), t=(dc == 31):
                        e.matmul(o, l, r, start=s, stop=t),
                        waits=w, inc=(dc == 31))
            trp = rope5(fb[bp][:, :], RBUF[:, rp, 0, :], RBUF[:, rp, 1, :],
                        STG[:, bp, 0:512], waits=[tmm, rbuf_tok, kstore[bp]])
            rope_tok[bp] = trp
            kstore[bp] = E("scalar", lambda e, o=K_scr[c, :, st * 512:(st + 1) * 512],
                           s=STG[:, bp, 0:512]: e.dma_start(out=o, in_=s),
                           waits=[trp], dma="sST%d" % bp)
        rbuf_war[rp] = (ENGSEM["vector"], p.cnt["dv"])
    p.barrier()
    if stop_after == "s1":
        return _finish()

    ab_stage = ABUF[:, :].rearrange("p (s c d) -> p s c d", s=2, c=32)
    tws = load_weight_cast(wv_v, wb3, 4, ["gW0", "gW1", "gmisc", "gW2"],
                           per_split=True)
    ev_tok = [None, None]
    vstore = [None, None]
    prev_mm = None
    x_cons = [None, None]
    pb_cons = [None, None]
    grp_st = [None, None]
    tx3 = [None, None]

    def s3a_ln(jj):
        sl = jj % 2
        txn = ln_full(XBUF[:, sl, :], PBUF[:, sl, :], PBUF[:, sl, :], sl,
                      waits=[tx3[sl], pb_cons[sl]])
        x_cons[sl] = txn
        return txn

    def s3a_tr(jj, txn):
        g, qq = jj // 4, jj % 4
        sl = jj % 2
        tpe, tac = transpose_chunks(
            lambda c, sl=sl: PBUF[:, sl, c * 128:(c + 1) * 128],
            lambda c, sl=sl: ab_stage[:, sl, c, :],
            32, txn, first_waits=[pb_cons[sl], grp_st[sl]])
        pb_cons[sl] = tpe
        grp_st[sl] = E("scalar",
                       lambda e, o=xnTo_scr[g, :, :, qq * 128:(qq + 1) * 128],
                       s=ab_stage[:, sl, :, :]: e.dma_start(out=o, in_=s),
                       waits=[tac], dma="sA%d" % sl)

    for st in range(8):
        tb_tok = E("sync", lambda e, o=tb32[:, :, :], s=xnT_scr[st, :, :, :]:
                   e.dma_start(out=o, in_=s), waits=[prev_mm], dma="lTB0")
        for dd in range(2):
            jj = 2 * st + dd
            sx = jj % 2
            tx3[sx] = E("sync", lambda e, o=XBUF[:, sx, :], s=x_own[jj, :, :]:
                        e.dma_start(out=o, in_=s), waits=[x_cons[sx]],
                        dma="lXB%d" % sx)
        txn3 = [s3a_ln(2 * st), None]
        if st == 0:
            txn3[1] = s3a_ln(1)
            s3a_tr(0, txn3[0])
            s3a_tr(1, txn3[1])
        for q in range(4):
            ti = st * 4 + q
            sl = q % 2
            for half in range(2):
                bp = half
                tmm = None
                for dc in range(32):
                    w = ()
                    if dc == 0:
                        w = [tb_tok, ev_tok[bp]]
                    if st == 0 and q == 0 and half == 0 and dc % 8 == 0:
                        w = list(w) + [tws[dc // 8]]
                    tmm = E("tensor", lambda e, o=fb[bp][:, :],
                            l=tb32[:, dc, q * 128:(q + 1) * 128],
                            r=wb3[:, dc, half * 512:(half + 1) * 512],
                            s=(dc == 0), t=(dc == 31):
                            e.matmul(o, l, r, start=s, stop=t),
                            waits=w, inc=(dc == 31))
                prev_mm = tmm
                ev_tok[bp] = E("scalar",
                               lambda e, o=STG[:, sl, half * 512:(half + 1) * 512],
                               i=fb[bp][:, :]: e.copy(out=o, in_=i),
                               waits=[tmm, vstore[sl]], inc=True)
            for kv in range(KV):
                vstore[sl] = E("scalar", lambda e, o=V_scr[kv, :, ti, :],
                               s=STG[:, sl, kv * 128:(kv + 1) * 128]:
                               e.dma_start(out=o, in_=s),
                               waits=[ev_tok[0], ev_tok[1]], dma="sST%d" % sl)
            if q == 1 and st > 0:
                txn3[1] = s3a_ln(2 * st + 1)
        if st > 0:
            s3a_tr(2 * st, txn3[0])
            s3a_tr(2 * st + 1, txn3[1])
    p.barrier()
    if stop_after in ("s2", "s3a"):
        return _finish()

    rope_tok = [None, None]
    qstore = [None, None]
    qproj_done = None
    rbuf_war = [None, None]
    for part in range(4):
        tw = load_weight_cast(wq_v[:, :, part * 1024:(part + 1) * 1024], wb3, 4,
                              "gW0", waits=[qproj_done])
        prev_g_mm = None
        for g in range(4):
            tb_tok = E("sync", lambda e, o=tb32[:, :, :], s=xnTo_scr[g, :, :, :]:
                       e.dma_start(out=o, in_=s), waits=[qproj_done, prev_g_mm],
                       dma="lTB0")
            rp = g % 2
            E("sync", lambda e, o=RBUF[:, rp, 0, :],
              s=cosq_t[:, g * 512:(g + 1) * 512]: e.dma_start(out=o, in_=s),
              waits=[rbuf_war[rp]], dma="lRB%d" % rp)
            rbuf_tok = E("sync", lambda e, o=RBUF[:, rp, 1, :],
                         s=sinq_t[:, g * 512:(g + 1) * 512]: e.dma_start(out=o, in_=s),
                         dma="lRB%d" % rp)
            for hh in range(8):
                h = part * 8 + hh
                kv, hq = h // 4, h % 4
                bp = hh % 2
                tmm = None
                for dc in range(32):
                    w = ()
                    if dc == 0:
                        w = [tb_tok, rope_tok[bp]] + (
                            [tw] if (g == 0 and hh == 0) else [])
                    tmm = E("tensor", lambda e, o=fb[bp][:, :],
                            l=wb3[:, dc, hh * 128:(hh + 1) * 128], r=tb32[:, dc, :],
                            s=(dc == 0), t=(dc == 31):
                            e.matmul(o, l, r, start=s, stop=t),
                            waits=w, inc=(dc == 31))
                prev_g_mm = tmm
                trp = rope5(fb[bp][:, :], RBUF[:, rp, 0, :], RBUF[:, rp, 1, :],
                            STG[:, bp, 0:512], waits=[tmm, rbuf_tok, qstore[bp]])
                rope_tok[bp] = trp
                qstore[bp] = E("scalar",
                               lambda e, o=Q_scr[kv, :, hq, 4 * g:4 * (g + 1), :],
                               s=STG[:, bp, 0:512].rearrange("p (j d) -> p j d", j=4):
                               e.dma_start(out=o, in_=s), waits=[trp],
                               dma="sST%d" % bp)
            rbuf_war[rp] = (ENGSEM["vector"], p.cnt["dv"])
        qproj_done = (ENGSEM["tensor"], p.cnt["pe"])
    p.barrier()
    if stop_after == "s3b":
        return _finish()

    ktb = TBUF[:, 0:4096]
    vtb = TBUF[:, 4096:8192].rearrange("p (t d) -> p t d", t=32)
    qtb = TBUF[:, 8192:16384].rearrange("p (h j d) -> p h j d", h=4, j=NSLOT)
    ab4 = ABUF[:, :].rearrange("p (j h d) -> p j h d", j=NSLOT, h=4)
    ptv = PBUF[:, :, :].rearrange("p s (g q) -> p s g q", g=8)

    attn_dma = None
    s4_pe_last = None
    ptw = [[None] * 8, [None] * 8]
    stp_war = [None, None]
    av_war = [None, None]
    sm_war = [None, None]
    bc_tok = [None, None]
    hc = 0
    last_attn_tok = None
    pend = None
    headq = []

    def flush_mult():
        nonlocal pend, last_attn_tok
        if pend is None:
            return False
        out0, pp0, avp0, bcs0, tbcc0, tav0 = pend
        tok = E("vector", lambda e, o=out0, a=avp0, b=bcs0:
                e.scalar_tensor_tensor(out=o, in0=a, scalar=c_one, in1=b,
                                       op0=ALU.mult, op1=ALU.mult),
                waits=[tbcc0, tav0, attn_dma], inc=True)
        av_war[pp0] = tok
        last_attn_tok = tok
        pend = None
        return True

    def flush_tail():
        nonlocal pend
        if not headq:
            return
        out0, pp0, avp0, bcs0, bcp0, rcp0, sums_row0, t_summ0, t_av0 = headq.pop(0)
        E("vector", lambda e, o=SUMS[:, :], i=sums_row0:
          e.tensor_reduce(out=o, in_=i, axis=AX.X, op=ALU.add),
          waits=[t_summ0])
        if not flush_mult():
            p.D("vector")
        def _recip(e, o=rcp0):
            with nc.allow_low_precision(reason="bf16 softmax-normalizer broadcast"):
                return e.reciprocal(out=o, in_=SUMS[:, :])
        trc = E("vector", _recip, waits=[bc_tok[pp0]], inc=True)
        tbc = E("tensor", lambda e, o=bcp0, r=rcp0: e.matmul(
            o, ORB[:, :], r, start=True, stop=True), waits=[trc], inc=True)
        bc_tok[pp0] = tbc
        tbcc = E("scalar", lambda e, o=bcs0, i=bcp0: e.copy(out=o, in_=i),
                 waits=[tbc, av_war[pp0]], inc=True)
        pend = (out0, pp0, avp0, bcs0, tbcc, t_av0)
        sm_war[pp0] = tbcc

    for kv in range(KV):
        E("sync", lambda e, o=ktb, s=K_scr[kv, :, :]: e.dma_start(out=o, in_=s),
          waits=[s4_pe_last], dma="lTB0")
        E("sync", lambda e, o=vtb, s=V_scr[kv, :, :, :]: e.dma_start(out=o, in_=s),
          dma="lTB0")
        tk_all = E("sync", lambda e, o=qtb, s=Q_scr[kv, :, :, :, :]:
                   e.dma_start(out=o, in_=s), dma="lTB0")
        for P in range(8):
            Epair = 4 * P + 4
            ngrp = Epair // 2
            for hp in range(2):
                heads = []
                for z in range(2):
                    hh = hp * 2 + z
                    pp = hc % 2
                    hc += 1
                    heads.append({
                        "hh": hh, "pp": pp, "z": z,
                        "q_ap": qtb[:, hh, 2 * P:2 * P + 2, :].rearrange(
                            "p j d -> p (j d)"),
                        "out_ap": ab4[:, 2 * P:2 * P + 2, hh, :],
                        "avp": fb[2 + pp][:, 0:256],
                        "bcp": fb[4 + pp][:, 0:256],
                        "rcp": RCP[:, pp, :],
                        "bcs": BCS[:, pp, :],
                        "t_exp": {}, "t_av": None,
                    })

                def emit_st(H, g, ngrp=ngrp):
                    bp = H["z"]
                    pp = H["pp"]
                    tm = None
                    for b2 in range(2):
                        b = 2 * g + b2
                        wt = [stp_war[bp], tk_all] if b2 == 0 else ()
                        tm = E("tensor", lambda e, o=fb[bp][:, b2 * 256:(b2 + 1) * 256],
                               l=ktb[:, b * 128:(b + 1) * 128], r=H["q_ap"],
                               s=(b2 == 0), t=(b2 == 1):
                               e.matmul(o, l, r, start=s, stop=t, skip_group_check=True),
                               waits=wt, inc=(b2 == 1))
                    tmk = tm
                    if g >= ngrp - 2:
                        r0 = 2 * (g - (ngrp - 2))
                        tmk = E("vector", lambda e, o=fb[bp][:, 0:512],
                                m=MSK2[:, r0:r0 + 2, :].rearrange("p a b -> p (a b)"):
                                e.tensor_tensor(out=o, in0=o, in1=m, op=ALU.min),
                                waits=[tm], inc=True)
                    H["t_exp"][g] = E("scalar",
                                      lambda e, o=ptv[:, pp, g % 8, :],
                                      i=fb[bp][:, 0:512]:
                                      e.activation(out=o, in_=i, func=AF.Exp,
                                                   bias=c_negb, scale=1.0),
                                      waits=[tmk, ptw[pp][g % 8]], inc=True)
                    stp_war[bp] = H["t_exp"][g]

                def emit_sums_av(H, g, ngrp=ngrp, Epair=Epair):
                    pp = H["pp"]
                    E("tensor", lambda e, o=fb[4 + pp][:, 0:512],
                      r=ptv[:, pp, g % 8, :],
                      s=(g == 0), t=(g == ngrp - 1):
                      e.matmul(o, OMAT[:, :], r, start=s, stop=t,
                               skip_group_check=True),
                      waits=[H["t_exp"][g]] + ([sm_war[pp]] if g == 0 else []),
                      inc=(g == ngrp - 1))
                    tm = None
                    for b2 in range(2):
                        b = 2 * g + b2
                        wt = [H["t_exp"][g]] + ([av_war[pp]] if b == 0 else [])
                        tm = E("tensor", lambda e, o=H["avp"], l=vtb[:, b, :],
                               r=ptv[:, pp, g % 8, b2 * 256:(b2 + 1) * 256],
                               s=(b == 0), t=(b == Epair - 1):
                               e.matmul(o, l, r, start=s, stop=t, skip_group_check=True),
                               waits=wt, inc=(b == Epair - 1))
                    ptw[pp][g % 8] = tm
                    H["t_av"] = tm

                A, Bh = heads
                emit_st(A, 0)
                emit_st(Bh, 0)
                flush_tail()
                flush_tail()
                flush_mult()
                for g in range(ngrp):
                    if g + 1 < ngrp:
                        emit_st(A, g + 1)
                    emit_sums_av(A, g)
                    if g + 1 < ngrp:
                        emit_st(Bh, g + 1)
                    emit_sums_av(Bh, g)
                for hd in heads:
                    pp = hd["pp"]
                    headq.append((hd["out_ap"], pp, hd["avp"], hd["bcs"], hd["bcp"],
                                  hd["rcp"],
                                  fb[4 + pp][0:1, 0:512].rearrange(
                                      "p (e q) -> p q e", e=2),
                                  (ENGSEM["tensor"], p.cnt["pe"]), hd["t_av"]))
        flush_tail()
        flush_tail()
        flush_mult()
        attn_dma = E("scalar", lambda e,
                     o=attn_scr[:, :, kv * 4:(kv + 1) * 4, :].rearrange(
                         "j p h d -> p j h d"),
                     s=ab4[:, :, :, :]: e.dma_start(out=o, in_=s),
                     waits=[last_attn_tok], dma="sAT")
        s4_pe_last = (ENGSEM["tensor"], p.cnt["pe"])
        if kv == 0:
            wo_pre = []
            for pr in range(2):
                t = None
                for i in range(2):
                    t = E("gpsimd", lambda e, o=wb4[:, i * 16:(i + 1) * 16, pr, :],
                          s=wo_v[:, i * 16:(i + 1) * 16, pr * 512:(pr + 1) * 512]:
                          e.dma_start(out=o, in_=s), dma="gW%d" % pr)
                wo_pre.append(t)
    p.barrier()
    if stop_after == "s4":
        return _finish()

    ab2 = ABUF[:, :].rearrange("p (s h d) -> p s h d", s=2, h=32)
    tb2 = TBUF[:, 0:8192].rearrange("p (s c d) -> p s c d", s=2, c=32)
    at_sb = TBUF[:, 8192:12288].rearrange("p (c r) -> p c r", c=32)
    bcb = TBUF[:, 12288:16384]
    hT = STG[:, :, :].rearrange("p a b -> p (a b)")
    load_weight_cast(a_v, at_sb, 1, "gmisc")
    E("gpsimd", lambda e: e.dma_start(out=bcb, in_=bc_f), dma="gmisc")
    ta_misc = E("gpsimd", lambda e: e.dma_start(out=BIAS[:, :], in_=bias_f),
                dma="gmisc")
    hT_toks = [None] * NSLOT
    s6state = {"slot_cons": [[], []], "pbuf1_cons": None,
               "txn2store": [None, None], "hT_prev": None}

    def s6_body(j, tsq_tok, ystore_tok):
        sl = j % 2
        sc6 = s6state
        ty = E("sync", lambda e, o=XBUF[:, sl, :], s=y_scr[j, :, :]:
               e.dma_start(out=o, in_=s),
               waits=sc6["slot_cons"][sl] + [ystore_tok], dma="lXB%d" % sl)
        sc6["slot_cons"][sl] = []
        tx = E("sync", lambda e, o=XBUF[:, 1 - sl, :], s=x_own[j, :, :]:
               e.dma_start(out=o, in_=s), waits=sc6["slot_cons"][1 - sl],
               dma="lXB%d" % (1 - sl))
        sc6["slot_cons"][1 - sl] = []
        L = LNS[:, sl, :]
        E("vector", lambda e, o=L[:, 0:1], i=LN2S[:, j, :]: e.tensor_reduce(
            out=o, in_=i, axis=AX.X, op=ALU.add), waits=[t_rstd[sl], tsq_tok])
        tcert = E("vector", lambda e, o=L[:, 2:3], i=LN2Q[:, j, :]: e.tensor_reduce(
            out=o, in_=i, axis=AX.X, op=ALU.add), inc=True, drain=True)
        mean, rstd, tr = ln_stats_fin(L[:, 0:1], L[:, 2:3], sl, [tcert])
        ty2 = E("vector", lambda e, o=XBUF[:, sl, :], m=mean, r=rstd: e.tensor_scalar(
            out=o, in0=o, scalar1=m, scalar2=r, op0=ALU.subtract, op1=ALU.mult),
          waits=[tr, ty], inc=True)
        t_lnuse[sl] = ty2
        p.D("vector")
        tx2 = E("vector", lambda e, o=XBUF[:, sl, :], b=XBUF[:, 1 - sl, :]:
                e.scalar_tensor_tensor(out=o, in0=o, scalar=c_one, in1=b,
                                       op0=ALU.mult, op1=ALU.add,
                                       accum_out=LNP[:, 2:3]),
                waits=[tx], inc=True, drain=True)
        sc6["slot_cons"][1 - sl].append(tx2)
        E("scalar", lambda e, o=PBUF[:, 0, :], i=XBUF[:, sl, :]:
          e.activation(out=o, in_=i, func=AF.Square, bias=0.0, scale=1.0,
                       accum_out=LNP[:, 3:4]), waits=[tx2])
        mean3, rstd3, tr3 = ln_stats_fin(LNP[:, 2:3], LNP[:, 3:4], sl, [tx2])
        txn2 = E("vector", lambda e, o=PBUF[:, 1, :], i=XBUF[:, sl, :], m=mean3,
                 r=rstd3: e.tensor_scalar(out=o, in0=i, scalar1=m, scalar2=r,
                                          op0=ALU.subtract, op1=ALU.mult),
                 waits=[tr3, sc6["pbuf1_cons"]], inc=True)
        t_lnuse[sl] = txn2
        sc6["slot_cons"][sl].append(txn2)
        tx2dma = E("scalar", lambda e, o=x2_scr[j, :, :], s=XBUF[:, sl, :]:
                   e.dma_start(out=o, in_=s), waits=[txn2], dma="sX2")
        sc6["slot_cons"][sl].append(tx2dma)
        tpe, tac = transpose_chunks(
            lambda c: PBUF[:, 1, c * 128:(c + 1) * 128],
            lambda c, sl=sl: tb2[:, sl, c, :],
            32, txn2, first_waits=[sc6["txn2store"][sl], sc6.get("hT_%d" % sl)])
        sc6["pbuf1_cons"] = tpe
        sc6["txn2store"][sl] = E("scalar", lambda e, o=xn2T_scr[j, :, :, :],
                                 s=tb2[:, sl, :, :]: e.dma_start(out=o, in_=s),
                                 waits=[tac], dma="sT2%d" % sl)
        bp2 = 2 + sl
        tmm = None
        for dc in range(32):
            w = ()
            if dc == 0:
                w = [tac, ta_misc, sc6["hT_prev"]]
            tmm = E("tensor", lambda e, o=fb[bp2][:, 0:128], l=at_sb[:, dc, :],
                    r=tb2[:, sl, dc, :], s=(dc == 0), t=(dc == 31):
                    e.matmul(o, l, r, start=s, stop=t), waits=w, inc=(dc == 31))
        th = E("vector", lambda e, o=hT[:, j * 128:(j + 1) * 128], i=fb[bp2][:, 0:128]:
               e.tensor_scalar(out=o, in0=i, scalar1=CO[:, 0:1], scalar2=None,
                               op0=ALU.mult), waits=[tmm], inc=True)
        hT_toks[j] = th
        sc6["hT_prev"] = th
        sc6["hT_%d" % sl] = tmm

    w_parts_done = [None, None]
    at_cons = [None, None]
    ev_done = [None, None]
    ystore = [None, None]
    for part in range(8):
        wsl = part % 2
        if part < 2:
            tw = wo_pre[part]
        else:
            tw = None
            for i in range(2):
                tw = E("gpsimd", lambda e, o=wb4[:, i * 16:(i + 1) * 16, wsl, :],
                       s=wo_v[:, i * 16:(i + 1) * 16, part * 512:(part + 1) * 512]:
                       e.dma_start(out=o, in_=s),
                       waits=[w_parts_done[wsl]] if i == 0 else (), dma="gW%d" % wsl)
        for j in range(NSLOT):
            sl = j % 2
            ta = E("sync", lambda e, o=ab2[:, sl, :, :], s=attn_scr[j, :, :, :]:
                   e.dma_start(out=o, in_=s), waits=[at_cons[sl]], dma="lAB%d" % sl)
            bp = j % 2
            tmm = None
            for dc in range(32):
                w = [ta, tw] + (ev_done[bp] or []) if dc == 0 else ()
                tmm = E("tensor", lambda e, o=fb[bp][:, :], l=ab2[:, sl, dc, :],
                        r=wb4[:, dc, wsl, :], s=(dc == 0), t=(dc == 31):
                        e.matmul(o, l, r, start=s, stop=t), waits=w, inc=(dc == 31))
            at_cons[sl] = tmm
            tev = E("vector", lambda e, o=YST[:, sl, :], i=fb[bp][:, :],
                    a=LN2S[:, j, part:part + 1]:
                    e.tensor_scalar(out=o, in0=i, scalar1=0.0, scalar2=None,
                                    op0=ALU.add, op1=ALU.add, accum_out=a),
                    waits=[tmm, ystore[sl]], inc=True)
            tsq = E("scalar", lambda e, o=PBUF[:, 0, 0:512], i=fb[bp][:, :],
                    a=LN2Q[:, j, part:part + 1]:
                    e.activation(out=o, in_=i, func=AF.Square, bias=0.0, scale=1.0,
                                 accum_out=a),
                    waits=[tev], inc=True)
            ev_done[bp] = [tev, tsq]
            ystore[sl] = E("scalar", lambda e, o=y_scr[j, :, part * 512:(part + 1) * 512],
                           s=YST[:, sl, :]: e.dma_start(out=o, in_=s),
                           waits=[tev], dma="sY%d" % sl)
            if part == 7:
                s6_body(j, tsq, ystore[sl])
        w_parts_done[wsl] = (ENGSEM["tensor"], p.cnt["pe"])
    p.barrier()
    if stop_after in ("s5", "s6"):
        return _finish()

    xt2 = TBUF[:, 0:8192].rearrange("p (s c d) -> p s c d", s=2, c=32)

    s8state = {"slot_cons": [[], []]}

    def s8_body(j, tsq_tok, fstore_tok):
        sl = j % 2
        sc8 = s8state
        tf = E("sync", lambda e, o=XBUF[:, sl, :], s=f_scr[j, :, :]:
               e.dma_start(out=o, in_=s), waits=sc8["slot_cons"][sl] + [fstore_tok],
               dma="lXB%d" % sl)
        sc8["slot_cons"][sl] = []
        tx2 = E("sync", lambda e, o=XBUF[:, 1 - sl, :], s=x2_scr[j, :, :]:
                e.dma_start(out=o, in_=s), waits=sc8["slot_cons"][1 - sl],
                dma="lXB%d" % (1 - sl))
        sc8["slot_cons"][1 - sl] = []
        L = LNS[:, sl, :]
        E("vector", lambda e, o=L[:, 0:1], i=LN4S[:, j, :]: e.tensor_reduce(
            out=o, in_=i, axis=AX.X, op=ALU.add), waits=[t_rstd[sl], tsq_tok])
        tcert = E("vector", lambda e, o=L[:, 2:3], i=LN4Q[:, j, :]: e.tensor_reduce(
            out=o, in_=i, axis=AX.X, op=ALU.add), inc=True, drain=True)
        mean, rstd, tr = ln_stats_fin(L[:, 0:1], L[:, 2:3], sl, [tcert])
        ty4 = E("vector", lambda e, o=XBUF[:, sl, :], m=mean, r=rstd: e.tensor_scalar(
            out=o, in0=o, scalar1=m, scalar2=r, op0=ALU.subtract, op1=ALU.mult),
          waits=[tr, tf], inc=True)
        t_lnuse[sl] = ty4
        p.D("vector")
        tout = E("vector", lambda e, o=XBUF[:, sl, :], b=XBUF[:, 1 - sl, :]:
                 e.tensor_tensor(out=o, in0=o, in1=b, op=ALU.add),
                 waits=[tx2], inc=True)
        sc8["slot_cons"][1 - sl].append(tout)
        todma = E("scalar", lambda e, o=out_own[j, :, :], s=XBUF[:, sl, :]:
                  e.dma_start(out=o, in_=s), waits=[tout], dma="sOUT")
        sc8["slot_cons"][sl].append(todma)

    xn2_cons = [None, None]
    w_parts_done = [None, None]
    ev_done = [None, None]
    fstore = [None, None]
    for part in range(8):
        wsl = part % 2
        tw = None
        for i in range(2):
            tw = E("gpsimd", lambda e, o=wb4[:, i * 16:(i + 1) * 16, wsl, :],
                   s=w_v[:, i * 16:(i + 1) * 16, part * 512:(part + 1) * 512]:
                   e.dma_start(out=o, in_=s),
                   waits=[w_parts_done[wsl]] if i == 0 else (), dma="gW%d" % wsl)
        for j in range(NSLOT):
            sl = j % 2
            txt = E("sync", lambda e, o=xt2[:, sl, :, :], s=xn2T_scr[j, :, :, :]:
                    e.dma_start(out=o, in_=s), waits=[xn2_cons[sl]], dma="lTB%d" % sl)
            bp = j % 2
            for dc in range(32):
                w = ()
                if dc == 0:
                    w = [txt, hT_toks[j], ta_misc, tw] + (ev_done[bp] or [])
                E("tensor", lambda e, o=fb[bp][:, :], l=xt2[:, sl, dc, :],
                  r=wb4[:, dc, wsl, :], s=(dc == 0):
                  e.matmul(o, l, r, start=s, stop=False), waits=w)
            E("tensor", lambda e, o=fb[bp][:, :], l=hT[:, j * 128:(j + 1) * 128],
              r=bcb[:, part * 512:(part + 1) * 512]:
              e.matmul(o, l, r, start=False, stop=False, skip_group_check=True))
            tmm = E("tensor", lambda e, o=fb[bp][:, :],
                    r=BIAS[:, part * 512:(part + 1) * 512]:
                    e.matmul(o, ORB[:, :], r, start=False, stop=True,
                             skip_group_check=True), inc=True)
            xn2_cons[sl] = tmm
            tev = E("vector", lambda e, o=YST[:, sl, :], i=fb[bp][:, :],
                    a=LN4S[:, j, part:part + 1]:
                    e.tensor_scalar(out=o, in0=i, scalar1=0.0, scalar2=None,
                                    op0=ALU.add, op1=ALU.add, accum_out=a),
                    waits=[tmm, fstore[sl]], inc=True)
            tsq = E("scalar", lambda e, o=PBUF[:, 0, 0:512], i=fb[bp][:, :],
                    a=LN4Q[:, j, part:part + 1]:
                    e.activation(out=o, in_=i, func=AF.Square, bias=0.0, scale=1.0,
                                 accum_out=a),
                    waits=[tev], inc=True)
            ev_done[bp] = [tev, tsq]
            fstore[sl] = E("scalar", lambda e, o=f_scr[j, :, part * 512:(part + 1) * 512],
                           s=YST[:, sl, :]: e.dma_start(out=o, in_=s),
                           waits=[tev], dma="sY%d" % sl)
            if part == 7:
                s8_body(j, tsq, fstore[sl])
        w_parts_done[wsl] = (ENGSEM["tensor"], p.cnt["pe"])
    return _finish()


def _host_tables():
    pos = np.arange(S, dtype=np.float64)[:, None]
    freqs = np.exp(-np.arange(0, HD, 2, dtype=np.float64) * (math.log(10000.0) / HD))
    ang = pos * freqs
    c = np.cos(ang).T.astype(np.float32)
    s = np.sin(ang).T.astype(np.float32)
    return np.vstack([c, c]), np.vstack([s, s])


def make_in_maps(L0_attn_hidden_states, L0_attn_Wq, L0_attn_Wk, L0_attn_Wv, L0_attn_Wo,
                 L0_ffn_lora_A, L0_ffn_lora_B, L0_ffn_W, L0_ffn_bias,
                 L0_ffn_lora_weights):
    x = np.ascontiguousarray(np.asarray(L0_attn_hidden_states, dtype=np.float32))
    wq = np.ascontiguousarray(np.asarray(L0_attn_Wq, dtype=np.float32).T)
    wk = np.ascontiguousarray(np.asarray(L0_attn_Wk, dtype=np.float32).T)
    wv = np.ascontiguousarray(np.asarray(L0_attn_Wv, dtype=np.float32).T)
    wo = np.ascontiguousarray(np.asarray(L0_attn_Wo, dtype=np.float32).T)
    wff = np.ascontiguousarray(np.asarray(L0_ffn_W, dtype=np.float32).T)
    la = np.asarray(L0_ffn_lora_A, dtype=np.float32)
    lb = np.asarray(L0_ffn_lora_B, dtype=np.float32)
    a_t = np.ascontiguousarray(la.reshape(128, D).T)
    bc = np.ascontiguousarray(lb.transpose(0, 2, 1).reshape(128, D))
    bias = np.ascontiguousarray(np.asarray(L0_ffn_bias, dtype=np.float32).reshape(1, D))
    wrep = np.ascontiguousarray(np.repeat(
        np.asarray(L0_ffn_lora_weights, dtype=np.float32), 16)[:, None])

    cosk, sink = _host_tables()
    sc = 1.0 / math.sqrt(HD)

    ccol = np.zeros((128, 8), dtype=np.float32)
    ccol[:, 0] = 1.0
    ccol[:, 1] = -1.0
    ccol[:, 2] = 2.0
    ccol[:, 3] = 1e-5
    ccol[:, 4] = 1.0 / D
    ccol[:, 5] = -BOUND
    ident = np.eye(128, dtype=ml_dtypes.bfloat16)
    ones_col = np.ones((128, 1), dtype=ml_dtypes.bfloat16)
    ones_row_f = np.ones((1, 128), dtype=np.float32)
    ones_row_bf = np.ones((1, 128), dtype=ml_dtypes.bfloat16)
    ones_mat = np.ones((128, 128), dtype=ml_dtypes.bfloat16)

    kk = np.arange(128)[:, None]
    qq = np.arange(128)[None, :]
    diag = np.where(qq >= kk, 1e9, -1e9).astype(np.float32)
    allv = np.full((128, 128), 1e9, dtype=np.float32)
    allm = np.full((128, 128), -1e9, dtype=np.float32)

    def pair_mask(o):
        m = np.empty((128, 4, 2, 128), dtype=np.float32)
        for rel in range(4):
            for s in range(2):
                d = rel - 2 * s - o
                m[:, rel, s, :] = allv if d < 0 else (diag if d == 0 else allm)
        return np.ascontiguousarray(m.reshape(128, 4 * 256))

    in_maps = []
    for c in range(8):
        b, o = c % 4, c // 4
        tiles = (2 * np.arange(NSLOT) + o) * 128
        idx = (tiles[:, None] + np.arange(128)[None, :]).ravel()
        x_ownc = np.ascontiguousarray(x[b][idx].reshape(NSLOT, 128, D))
        cosq = np.ascontiguousarray(cosk[:, idx] * sc)
        sinq = np.ascontiguousarray(sink[:, idx] * sc)
        mask = pair_mask(o)
        in_maps.append({
            "x_own": x_ownc, "x_kv": x[b],
            "wq_t": wq, "wk_t": wk, "wv_t": wv, "wo_t": wo, "w_t": wff,
            "a_t": a_t, "bc_f": bc, "bias_f": bias, "wrep": wrep,
            "cosk_t": cosk, "sink_t": sink, "cosq_t": cosq, "sinq_t": sinq,
            "maskmin": np.ascontiguousarray(mask), "ccol": ccol,
            "ident": ident, "ones_col": ones_col, "ones_mat": ones_mat,
            "ones_row_f": ones_row_f, "ones_row_bf": ones_row_bf,
        })
    return in_maps


def kernel(**inputs):
    in_maps = make_in_maps(**inputs)
    nc = build_program()
    res = run_bass_kernel_spmd(nc, in_maps, core_ids=list(range(8)))

    out = np.empty((B, S, D), dtype=np.float32)
    for c in range(8):
        b, o = c % 4, c // 4
        oo = res.results[c]["out_own"]
        for j in range(NSLOT):
            t0 = (2 * j + o) * 128
            out[b, t0:t0 + 128, :] = oo[j]
    return out


# version 16
# speedup vs baseline: 1.0339x; 1.0339x over previous
import math
from contextlib import ExitStack

import numpy as np
import ml_dtypes

import concourse.bass as bass
import concourse.mybir as mybir
from concourse.bass_utils import run_bass_kernel_spmd

F32 = mybir.dt.float32
BF16 = mybir.dt.bfloat16
AF = mybir.ActivationFunctionType
ALU = mybir.AluOpType
AX = mybir.AxisListType

B, S, D = 4, 4096, 4096
H, KV, HD = 32, 8, 128
NSLOT = 16
BOUND = 40.0

ENGS = ("sync", "gpsimd", "scalar", "vector", "tensor")
ENGSEM = {"tensor": "pe", "vector": "dv", "scalar": "ac", "gpsimd": "gp", "sync": None}
QSEM = {"sync": "qL", "gpsimd": "qG", "scalar": "qS"}
SEMS = ("qL", "qG", "qS", "pe", "dv", "ac", "gp")


class Plan:

    def __init__(self):
        self.streams = {e: [] for e in ENGS}
        self.cnt = {}
        self.waited = {e: {} for e in ENGS}
        self.pending = {e: [] for e in ENGS}
        self.sem_owner = {}
        for s in ("pe", "dv", "ac", "gp", "sy"):
            self.cnt[s] = 0

    def E(self, eng, fn, waits=(), inc=False, dma=None, drain=False):
        wl = list(self.pending[eng])
        self.pending[eng] = []
        wl += [w for w in waits if w is not None]
        eff = []
        own = ENGSEM[eng]
        for (s, v) in wl:
            if s == own:
                continue
            if v <= self.waited[eng].get(s, 0):
                continue
            self.waited[eng][s] = v
            eff.append((s, v))
        incs = []
        tok = None
        if dma is not None:
            assert eng in QSEM, eng
            self.sem_owner.setdefault(dma, eng)
            assert self.sem_owner[dma] == eng, (dma, eng)
            self.cnt[dma] = self.cnt.get(dma, 0) + 16
            incs.append((dma, 16))
            tok = (dma, self.cnt[dma])
        elif inc and not drain:
            self.cnt[own] += 1
            incs.append((own, 1))
            tok = (own, self.cnt[own])
        self.streams[eng].append([eff, fn, incs])
        if drain:
            dincs = []
            if inc:
                self.cnt[own] += 1
                dincs.append((own, 1))
                tok = (own, self.cnt[own])
            self.streams[eng].append([[], lambda e: e.drain(), dincs])
        return tok

    def D(self, eng):
        self.streams[eng].append([[], lambda e: e.drain(), []])

    def barrier(self):
        toks = []
        for e in ENGS:
            st = self.streams[e]
            if not st:
                continue
            eff, fn, incs = st[-1]
            own = ENGSEM[e] or "gp"
            if incs:
                toks.append((incs[-1][0], self.cnt[incs[-1][0]]))
            else:
                self.cnt[own] += 1
                incs.append((own, 1))
                toks.append((own, self.cnt[own]))
            if e in QSEM:
                drains = [(s, self.cnt[s]) for s, o in self.sem_owner.items()
                          if o == e and self.cnt[s] > self.waited[e].get(s, 0)]
                if drains:
                    self.E(e, lambda en: en.nop(), waits=drains)
                    own2 = {"sync": "sy", "gpsimd": "gp", "scalar": "ac"}[e]
                    self.cnt[own2] = self.cnt.get(own2, 0) + 1
                    self.streams[e][-1][2].append((own2, 1))
                    toks.append((own2, self.cnt[own2]))
        for e2 in ENGS:
            self.pending[e2] += toks


def ceil_div(a, b):
    return -(-a // b)


def build_program(dbg=False, stop_after="s8"):
    nc = bass.Bass("TRN2", target_bir_lowering=False)

    def inp(name, shape, dt=F32):
        return nc.dram_tensor(name, shape, dt, kind="ExternalInput").ap()

    x_own = inp("x_own", [NSLOT, 128, D])
    x_kv = inp("x_kv", [S, D])
    wq_t = inp("wq_t", [D, D])
    wk_t = inp("wk_t", [D, KV * HD])
    wv_t = inp("wv_t", [D, KV * HD])
    wo_t = inp("wo_t", [D, D])
    w_t = inp("w_t", [D, D])
    a_t = inp("a_t", [D, 128])
    bc_f = inp("bc_f", [128, D])
    bias_f = inp("bias_f", [1, D])
    wrep = inp("wrep", [128, 1])
    cosk_t = inp("cosk_t", [128, S])
    sink_t = inp("sink_t", [128, S])
    cosq_t = inp("cosq_t", [128, NSLOT * 128])
    sinq_t = inp("sinq_t", [128, NSLOT * 128])
    maskmin = inp("maskmin", [128, 4 * 256])
    ccol = inp("ccol", [128, 8])
    ident_i = inp("ident", [128, 128], BF16)
    ones_col_i = inp("ones_col", [128, 1], BF16)
    ones_row_f_i = inp("ones_row_f", [1, 128])
    ones_row_bf_i = inp("ones_row_bf", [1, 128], BF16)
    ones_mat_i = inp("ones_mat", [128, 128], BF16)

    out_own = nc.dram_tensor("out_own", [NSLOT, 128, D], F32, kind="ExternalOutput").ap()

    def scr(name, shape, dt):
        kind = "ExternalOutput" if dbg else "Internal"
        return nc.dram_tensor(name, shape, dt, kind=kind).ap()

    xnT_scr = scr("xnT_scr", [8, 128, 32, 512], BF16)
    xnTo_scr = scr("xnTo_scr", [4, 128, 32, 512], BF16)
    K_scr = scr("K_scr", [KV, 128, S], BF16)
    V_scr = scr("V_scr", [KV, 128, 32, 128], BF16)
    Q_scr = scr("Q_scr", [KV, 128, 4, NSLOT, 128], BF16)
    attn_scr = scr("attn_scr", [NSLOT, 128, H, 128], BF16)
    y_scr = scr("y_scr", [NSLOT, 128, D], F32)
    x2_scr = scr("x2_scr", [NSLOT, 128, D], F32)
    xn2T_scr = scr("xn2T_scr", [NSLOT, 128, 32, 128], BF16)
    f_scr = scr("f_scr", [NSLOT, 128, D], F32)
    if dbg and dbg >= 2:
        dbg_sums = scr("dbg_sums", [KV, NSLOT, 4, 128], F32)
        dbg_rcp = scr("dbg_rcp", [KV, NSLOT, 4, 128], F32)
        dbg_pt = scr("dbg_pt", [NSLOT, 128, D], BF16)

    al = nc.alloc_sbuf_tensor
    WBUF = al("WBUF", [128, 32, 1024], BF16)
    TBUF = al("TBUF", [128, 16384], BF16)
    XBUF = al("XBUF", [128, 2, 4096], F32)
    PBUF = al("PBUF", [128, 2, 4096], BF16)
    ABUF = al("ABUF", [128, 8192], BF16)
    STG = al("STG", [128, 2, 1024], BF16)
    RBUF = al("RBUF", [128, 2, 2, 512], F32)
    RTMP = al("RTMP", [128, 2, 512], F32)
    MSK2 = al("MSK2", [128, 4, 256], F32)
    LNS = al("LNS", [128, 2, 8], F32)
    LNP = al("LNP", [128, 8], F32)
    LN2S = al("LN2S", [128, NSLOT, 8], F32)
    LN2Q = al("LN2Q", [128, NSLOT, 8], F32)
    LN4S = al("LN4S", [128, NSLOT, 8], F32)
    LN4Q = al("LN4Q", [128, NSLOT, 8], F32)
    CC = al("CC", [128, 8], F32)
    CO = al("CO", [128, 1], F32)
    WREP = al("WREP", [128, 1], F32)
    IDT = al("IDT", [128, 128], BF16)
    OCOL = al("OCOL", [128, 1], BF16)
    ORF = al("ORF", [1, 128], F32)
    ORB = al("ORB", [1, 128], BF16)
    BIAS = al("BIAS", [1, D], BF16)
    SUMS = al("SUMS", [1, 256], F32)
    RCP = al("RCP", [1, 2, 256], BF16)
    OMAT = al("OMAT", [128, 128], BF16)
    YST = al("YST", [128, 2, 512], F32)
    BCS = al("BCS", [128, 2, 256], F32)

    c_one = CC[:, 0:1]
    c_two = CC[:, 2:3]
    c_eps = CC[:, 3:4]
    c_invdm = CC[:, 4:5]
    c_negb = CC[:, 5:6]
    nc.const_aps.register(F32, 1e-5, c_eps)

    fb = [nc.alloc_psum_tensor(f"fb{i}", [128, 512], F32) for i in range(6)]
    tb_ps = [nc.alloc_psum_tensor(f"tp{i}", [128, 128], BF16) for i in range(2)]

    p = Plan()
    E = p.E

    t_rstd = [None, None]
    t_lnuse = [None, None]

    def ln_stats_fin(sum_ap, ssq_ap, par, waits):
        L = LNS[:, par, :]
        E("scalar", lambda e, o=L[:, 1:2], i=sum_ap: e.activation(
            out=o, in_=i, func=AF.Copy, bias=0.0, scale=c_invdm),
          waits=list(waits) + [t_lnuse[par]])
        E("scalar", lambda e, o=L[:, 0:1], i=ssq_ap: e.activation(
            out=o, in_=i, func=AF.Copy, bias=0.0, scale=c_invdm))
        E("scalar", lambda e, o=L[:, 3:4], i=L[:, 1:2]: e.activation(
            out=o, in_=i, func=AF.Square, bias=0.0, scale=1.0))
        E("scalar", lambda e, o=L[:, 4:5], i=L[:, 3:4]: e.activation(
            out=o, in_=i, func=AF.Copy, bias=1e-5, scale=-1.0))
        E("scalar", lambda e, o=L[:, 6:7], i=L[:, 4:5]: e.activation(
            out=o, in_=i, func=AF.Copy, bias=0.0, scale=1.0))
        E("scalar", lambda e, o=L[:, 5:6], i=L[:, 6:7]: e.add(
            out=o, in_=i, add=L[:, 0:1]))
        E("scalar", lambda e, o=L[:, 6:7], i=L[:, 5:6]: e.activation(
            out=o, in_=i, func=AF.Ln, bias=0.0, scale=1.0))
        tr = E("scalar", lambda e, o=L[:, 7:8], i=L[:, 6:7]: e.activation(
            out=o, in_=i, func=AF.Exp, bias=0.0, scale=-0.5),
            inc=True, drain=True)
        t_rstd[par] = tr
        return L[:, 1:2], L[:, 7:8], tr

    def ln_full(x_ap, xn_ap, scrap_ap, par, waits, scrap_war=None):
        L = LNS[:, par, :]
        tcert = E("vector", lambda e, o=L[:, 0:1], i=x_ap: e.tensor_reduce(
            out=o, in_=i, axis=AX.X, op=ALU.add),
          waits=list(waits) + [t_rstd[par]], inc=True, drain=True)
        E("scalar", lambda e, o=scrap_ap, i=x_ap, a=L[:, 2:3]:
          e.activation(out=o, in_=i, func=AF.Square, bias=0.0, scale=1.0,
                       accum_out=a), waits=list(waits))
        mean, rstd, tr = ln_stats_fin(L[:, 0:1], L[:, 2:3], par, [tcert])
        txn = E("vector", lambda e, o=xn_ap, i=x_ap, m=mean, r=rstd: e.tensor_scalar(
            out=o, in0=i, scalar1=m, scalar2=r, op0=ALU.subtract, op1=ALU.mult),
          waits=[tr], inc=True)
        t_lnuse[par] = txn
        return txn

    tr_war = [None, None]

    def transpose_chunks(src_ap_fn, dst_ap_fn, n, t_src, first_waits=()):
        tpe = tac = None
        for c in range(n):
            par = c % 2
            w = [t_src, tr_war[par]]
            if c == 0:
                w += list(first_waits)
            tpe = E("tensor", lambda e, o=tb_ps[par][:, :], i=src_ap_fn(c): e.transpose(
                o, i, IDT[:, :]), waits=w, inc=True)
            tac = E("scalar", lambda e, o=dst_ap_fn(c), i=tb_ps[par][:, :]: e.copy(
                out=o, in_=i), waits=[tpe], inc=True)
            tr_war[par] = tac
        return tpe, tac

    def rope5(ps_ap, cos_ap, sin_ap, out_ap, waits):
        t1 = RTMP[:, 0, :]
        u = RTMP[:, 1, :]
        E("vector", lambda e, o=t1, a=ps_ap, b=cos_ap: e.tensor_tensor(
            out=o, in0=a, in1=b, op=ALU.mult), waits=waits)
        E("vector", lambda e, o=u[0:64, :], a=ps_ap[64:128, :], b=sin_ap[0:64, :]:
          e.tensor_tensor(out=o, in0=a, in1=b, op=ALU.mult))
        E("vector", lambda e, o=u[64:128, :], a=ps_ap[0:64, :], b=sin_ap[64:128, :]:
          e.tensor_tensor(out=o, in0=a, in1=b, op=ALU.mult))
        E("vector", lambda e, o=out_ap[0:64, :], a=t1[0:64, :], b=u[0:64, :]:
          e.tensor_tensor(out=o, in0=a, in1=b, op=ALU.subtract))
        tk = E("vector", lambda e, o=out_ap[64:128, :], a=t1[64:128, :], b=u[64:128, :]:
               e.tensor_tensor(out=o, in0=a, in1=b, op=ALU.add), inc=True)
        return tk

    def load_weight_cast(dram_ap_3d, dst_ap_3d, nsplit, sem, waits=(),
                         per_split=False):
        C = dram_ap_3d.shape[1]
        step = C // nsplit
        toks = []
        for i in range(nsplit):
            w = list(waits) if i == 0 else ()
            sm = sem[i] if per_split else sem
            toks.append(E("gpsimd",
                          lambda e, o=dst_ap_3d[:, i * step:(i + 1) * step, :],
                          s=dram_ap_3d[:, i * step:(i + 1) * step, :]:
                          e.dma_start(out=o, in_=s), waits=w, dma=sm))
        return toks if per_split else toks[-1]

    def _finish():
        p.barrier()
        E("gpsimd", lambda e: e.nop())
        n_ins = {e: len(p.streams[e]) for e in ENGS}
        print("kernel build: instructions per engine:", n_ins,
              "total", sum(n_ins.values()),
              "sbuf_remaining", nc.sbuf_bytes_remaining,
              "psum_banks_remaining", nc.psum_banks_remaining)
        with ExitStack() as ctx:
            sems = {n: ctx.enter_context(nc.semaphore(n)) for n in p.cnt}
            block = ctx.enter_context(nc.Block())

            def make_replayer(stream):
                def f(eng):
                    for eff, fn, incs in stream:
                        for (s, v) in eff:
                            eng.wait_ge(sems[s], v)
                        ins = fn(eng)
                        for (s, a) in incs:
                            ins.then_inc(sems[s], a)
                return f

            block.sync(make_replayer(p.streams["sync"]))
            block.gpsimd(make_replayer(p.streams["gpsimd"]))
            block.scalar(make_replayer(p.streams["scalar"]))
            block.vector(make_replayer(p.streams["vector"]))
            block.tensor(make_replayer(p.streams["tensor"]))
        return nc

    E("sync", lambda e: e.dma_start(out=CC[:, :], in_=ccol), dma="lpre")
    E("sync", lambda e: e.dma_start(out=IDT[:, :], in_=ident_i), dma="lpre")
    E("sync", lambda e: e.dma_start(out=OCOL[:, :], in_=ones_col_i), dma="lpre")
    E("sync", lambda e: e.dma_start(out=ORF[:, :], in_=ones_row_f_i), dma="lpre")
    E("sync", lambda e: e.dma_start(out=ORB[:, :], in_=ones_row_bf_i), dma="lpre")
    E("sync", lambda e: e.dma_start(out=MSK2[:, :, :], in_=maskmin.rearrange(
        "p (a b) -> p a b", a=4)), dma="lpre")
    E("sync", lambda e: e.dma_start(out=OMAT[:, :], in_=ones_mat_i), dma="lpre")
    tpre = E("sync", lambda e: e.dma_start(out=WREP[:, :], in_=wrep), dma="lpre")
    tcc = tpre
    E("vector", lambda e: e.tensor_tensor(out=CO[:, :], in0=WREP[:, :], in1=c_two,
                                          op=ALU.mult), waits=[tpre])

    wk_v = wk_t.rearrange("(c p) m -> p c m", p=128)
    wv_v = wv_t.rearrange("(c p) m -> p c m", p=128)
    wq_v = wq_t.rearrange("(c p) m -> p c m", p=128)
    wo_v = wo_t.rearrange("(c p) m -> p c m", p=128)
    w_v = w_t.rearrange("(c p) m -> p c m", p=128)
    a_v = a_t.rearrange("(c p) m -> p c m", p=128)

    wb3 = WBUF[:, :, :]
    wb4 = WBUF[:, :, :].rearrange("p c (s m) -> p c s m", s=2)
    tb32 = TBUF[:, :].rearrange("p (c t) -> p c t", c=32)

    tw = load_weight_cast(wk_v, wb3, 4, "gW0")
    x_cons = [None, None]
    pb_cons = [None, None]
    rope_tok = [None, None]
    kstore = [None, None]
    rbuf_war = [None, None]
    scrap_tok = None
    tbuf_store = None

    for st in range(8):
        rp = st % 2
        E("sync", lambda e, o=RBUF[:, rp, 0, :],
          s=cosk_t[:, st * 512:(st + 1) * 512]: e.dma_start(out=o, in_=s),
          waits=[rbuf_war[rp]], dma="lRB%d" % rp)
        rbuf_tok = E("sync", lambda e, o=RBUF[:, rp, 1, :],
                     s=sink_t[:, st * 512:(st + 1) * 512]: e.dma_start(out=o, in_=s),
                     dma="lRB%d" % rp)
        ac_last = None
        for q in range(4):
            ti = st * 4 + q
            sl = q % 2
            tx = E("sync", lambda e, o=XBUF[:, sl, :],
                   s=x_kv[ti * 128:(ti + 1) * 128, :]: e.dma_start(out=o, in_=s),
                   waits=[x_cons[sl], tcc], dma="lXB%d" % sl)
            txn = ln_full(XBUF[:, sl, :], PBUF[:, sl, :], ABUF[:, 0:4096], sl,
                          waits=[tx], scrap_war=scrap_tok)
            scrap_tok = txn
            x_cons[sl] = txn
            tpe, tac = transpose_chunks(
                lambda c, sl=sl: PBUF[:, sl, c * 128:(c + 1) * 128],
                lambda c, q=q: tb32[:, c, q * 128:(q + 1) * 128],
                32, txn, first_waits=[pb_cons[sl], tbuf_store])
            pb_cons[sl] = tpe
            ac_last = tac
        tbuf_store = E("scalar", lambda e, o=xnT_scr[st, :, :, :], s=tb32[:, :, :]:
                       e.dma_start(out=o, in_=s), waits=[ac_last], dma="sTB")
        for c in range(KV):
            bp = c % 2
            tmm = None
            for dc in range(32):
                w = ()
                if dc == 0:
                    w = [ac_last, rope_tok[bp]] + ([tw] if (st == 0 and c == 0) else [])
                tmm = E("tensor", lambda e, o=fb[bp][:, :],
                        l=wb3[:, dc, c * 128:(c + 1) * 128], r=tb32[:, dc, :],
                        s=(dc == 0), t=(dc == 31):
                        e.matmul(o, l, r, start=s, stop=t),
                        waits=w, inc=(dc == 31))
            trp = rope5(fb[bp][:, :], RBUF[:, rp, 0, :], RBUF[:, rp, 1, :],
                        STG[:, bp, 0:512], waits=[tmm, rbuf_tok, kstore[bp]])
            rope_tok[bp] = trp
            kstore[bp] = E("scalar", lambda e, o=K_scr[c, :, st * 512:(st + 1) * 512],
                           s=STG[:, bp, 0:512]: e.dma_start(out=o, in_=s),
                           waits=[trp], dma="sST%d" % bp)
        rbuf_war[rp] = (ENGSEM["vector"], p.cnt["dv"])
    p.barrier()
    if stop_after == "s1":
        return _finish()

    ab_stage = ABUF[:, :].rearrange("p (s c d) -> p s c d", s=2, c=32)
    tws = load_weight_cast(wv_v, wb3, 4, ["gW0", "gW1", "gmisc", "gW2"],
                           per_split=True)
    ev_tok = [None, None]
    vstore = [None, None]
    prev_mm = None
    x_cons = [None, None]
    pb_cons = [None, None]
    grp_st = [None, None]
    tx3 = [None, None]

    def s3a_ln(jj):
        sl = jj % 2
        txn = ln_full(XBUF[:, sl, :], PBUF[:, sl, :], PBUF[:, sl, :], sl,
                      waits=[tx3[sl], pb_cons[sl]])
        x_cons[sl] = txn
        return txn

    def s3a_tr(jj, txn):
        g, qq = jj // 4, jj % 4
        sl = jj % 2
        tpe, tac = transpose_chunks(
            lambda c, sl=sl: PBUF[:, sl, c * 128:(c + 1) * 128],
            lambda c, sl=sl: ab_stage[:, sl, c, :],
            32, txn, first_waits=[pb_cons[sl], grp_st[sl]])
        pb_cons[sl] = tpe
        grp_st[sl] = E("scalar",
                       lambda e, o=xnTo_scr[g, :, :, qq * 128:(qq + 1) * 128],
                       s=ab_stage[:, sl, :, :]: e.dma_start(out=o, in_=s),
                       waits=[tac], dma="sA%d" % sl)

    for st in range(8):
        tb_tok = E("sync", lambda e, o=tb32[:, :, :], s=xnT_scr[st, :, :, :]:
                   e.dma_start(out=o, in_=s), waits=[prev_mm], dma="lTB0")
        for dd in range(2):
            jj = 2 * st + dd
            sx = jj % 2
            tx3[sx] = E("sync", lambda e, o=XBUF[:, sx, :], s=x_own[jj, :, :]:
                        e.dma_start(out=o, in_=s), waits=[x_cons[sx]],
                        dma="lXB%d" % sx)
        txn3 = [s3a_ln(2 * st), None]
        if st == 0:
            txn3[1] = s3a_ln(1)
            s3a_tr(0, txn3[0])
            s3a_tr(1, txn3[1])
        for q in range(4):
            ti = st * 4 + q
            sl = q % 2
            for half in range(2):
                bp = half
                tmm = None
                for dc in range(32):
                    w = ()
                    if dc == 0:
                        w = [tb_tok, ev_tok[bp]]
                    if st == 0 and q == 0 and half == 0 and dc % 8 == 0:
                        w = list(w) + [tws[dc // 8]]
                    tmm = E("tensor", lambda e, o=fb[bp][:, :],
                            l=tb32[:, dc, q * 128:(q + 1) * 128],
                            r=wb3[:, dc, half * 512:(half + 1) * 512],
                            s=(dc == 0), t=(dc == 31):
                            e.matmul(o, l, r, start=s, stop=t),
                            waits=w, inc=(dc == 31))
                prev_mm = tmm
                ev_tok[bp] = E("scalar",
                               lambda e, o=STG[:, sl, half * 512:(half + 1) * 512],
                               i=fb[bp][:, :]: e.copy(out=o, in_=i),
                               waits=[tmm, vstore[sl]], inc=True)
            for kv in range(KV):
                vstore[sl] = E("scalar", lambda e, o=V_scr[kv, :, ti, :],
                               s=STG[:, sl, kv * 128:(kv + 1) * 128]:
                               e.dma_start(out=o, in_=s),
                               waits=[ev_tok[0], ev_tok[1]], dma="sST%d" % sl)
            if q == 1 and st > 0:
                txn3[1] = s3a_ln(2 * st + 1)
        if st > 0:
            s3a_tr(2 * st, txn3[0])
            s3a_tr(2 * st + 1, txn3[1])

    def wq_half_load(part, half, waits):
        return load_weight_cast(
            wq_v[:, :, part * 1024 + half * 512:part * 1024 + (half + 1) * 512],
            wb4[:, :, half, :], 2, "gW%d" % half, waits=waits)

    half_ld = [wq_half_load(0, h, waits=[prev_mm]) for h in range(2)]
    p.barrier()
    if stop_after in ("s2", "s3a"):
        return _finish()

    rope_tok = [None, None]
    qstore = [None, None]
    qproj_done = None
    rbuf_war = [None, None]
    half_rd = [None, None]
    for part in range(4):
        prev_g_mm = None
        for g in range(4):
            tb_tok = E("sync", lambda e, o=tb32[:, :, :], s=xnTo_scr[g, :, :, :]:
                       e.dma_start(out=o, in_=s), waits=[qproj_done, prev_g_mm],
                       dma="lTB0")
            rp = g % 2
            E("sync", lambda e, o=RBUF[:, rp, 0, :],
              s=cosq_t[:, g * 512:(g + 1) * 512]: e.dma_start(out=o, in_=s),
              waits=[rbuf_war[rp]], dma="lRB%d" % rp)
            rbuf_tok = E("sync", lambda e, o=RBUF[:, rp, 1, :],
                         s=sinq_t[:, g * 512:(g + 1) * 512]: e.dma_start(out=o, in_=s),
                         dma="lRB%d" % rp)
            for hh in range(8):
                h = part * 8 + hh
                kv, hq = h // 4, h % 4
                hf = hh // 4
                bp = hh % 2
                tmm = None
                for dc in range(32):
                    w = ()
                    if dc == 0:
                        w = [tb_tok, rope_tok[bp]] + (
                            [half_ld[hf]] if (g == 0 and hh % 4 == 0) else [])
                    tmm = E("tensor", lambda e, o=fb[bp][:, :],
                            l=wb4[:, dc, hf, (hh % 4) * 128:(hh % 4 + 1) * 128],
                            r=tb32[:, dc, :],
                            s=(dc == 0), t=(dc == 31):
                            e.matmul(o, l, r, start=s, stop=t),
                            waits=w, inc=(dc == 31))
                prev_g_mm = tmm
                if g == 3 and hh % 4 == 3:
                    half_rd[hf] = tmm
                trp = rope5(fb[bp][:, :], RBUF[:, rp, 0, :], RBUF[:, rp, 1, :],
                            STG[:, bp, 0:512], waits=[tmm, rbuf_tok, qstore[bp]])
                rope_tok[bp] = trp
                qstore[bp] = E("scalar",
                               lambda e, o=Q_scr[kv, :, hq, 4 * g:4 * (g + 1), :],
                               s=STG[:, bp, 0:512].rearrange("p (j d) -> p j d", j=4):
                               e.dma_start(out=o, in_=s), waits=[trp],
                               dma="sST%d" % bp)
            rbuf_war[rp] = (ENGSEM["vector"], p.cnt["dv"])
        qproj_done = (ENGSEM["tensor"], p.cnt["pe"])
        if part < 3:
            half_ld = [wq_half_load(part + 1, h, waits=[half_rd[h]])
                       for h in range(2)]
    p.barrier()
    if stop_after == "s3b":
        return _finish()

    ktb = TBUF[:, 0:4096]
    vtb = TBUF[:, 4096:8192].rearrange("p (t d) -> p t d", t=32)
    qtb = TBUF[:, 8192:16384].rearrange("p (h j d) -> p h j d", h=4, j=NSLOT)
    ab4 = ABUF[:, :].rearrange("p (j h d) -> p j h d", j=NSLOT, h=4)
    ptv = PBUF[:, :, :].rearrange("p s (g q) -> p s g q", g=8)

    attn_dma = None
    s4_pe_last = None
    ptw = [[None] * 8, [None] * 8]
    stp_war = [None, None]
    av_war = [None, None]
    sm_war = [None, None]
    bc_tok = [None, None]
    hc = 0
    last_attn_tok = None
    pend = None
    headq = []

    def flush_mult():
        nonlocal pend, last_attn_tok
        if pend is None:
            return False
        out0, pp0, avp0, bcs0, tbcc0, tav0 = pend
        tok = E("vector", lambda e, o=out0, a=avp0, b=bcs0:
                e.scalar_tensor_tensor(out=o, in0=a, scalar=c_one, in1=b,
                                       op0=ALU.mult, op1=ALU.mult),
                waits=[tbcc0, tav0, attn_dma], inc=True)
        av_war[pp0] = tok
        last_attn_tok = tok
        pend = None
        return True

    def flush_tail():
        nonlocal pend
        if not headq:
            return
        out0, pp0, avp0, bcs0, bcp0, rcp0, sums_row0, t_summ0, t_av0 = headq.pop(0)
        E("vector", lambda e, o=SUMS[:, :], i=sums_row0:
          e.tensor_reduce(out=o, in_=i, axis=AX.X, op=ALU.add),
          waits=[t_summ0])
        if not flush_mult():
            p.D("vector")
        def _recip(e, o=rcp0):
            with nc.allow_low_precision(reason="bf16 softmax-normalizer broadcast"):
                return e.reciprocal(out=o, in_=SUMS[:, :])
        trc = E("vector", _recip, waits=[bc_tok[pp0]], inc=True)
        tbc = E("tensor", lambda e, o=bcp0, r=rcp0: e.matmul(
            o, ORB[:, :], r, start=True, stop=True), waits=[trc], inc=True)
        bc_tok[pp0] = tbc
        tbcc = E("scalar", lambda e, o=bcs0, i=bcp0: e.copy(out=o, in_=i),
                 waits=[tbc, av_war[pp0]], inc=True)
        pend = (out0, pp0, avp0, bcs0, tbcc, t_av0)
        sm_war[pp0] = tbcc

    for kv in range(KV):
        E("sync", lambda e, o=ktb, s=K_scr[kv, :, :]: e.dma_start(out=o, in_=s),
          waits=[s4_pe_last], dma="lTB0")
        E("sync", lambda e, o=vtb, s=V_scr[kv, :, :, :]: e.dma_start(out=o, in_=s),
          dma="lTB0")
        tk_all = E("sync", lambda e, o=qtb, s=Q_scr[kv, :, :, :, :]:
                   e.dma_start(out=o, in_=s), dma="lTB0")
        for P in range(8):
            Epair = 4 * P + 4
            ngrp = Epair // 2
            for hp in range(2):
                heads = []
                for z in range(2):
                    hh = hp * 2 + z
                    pp = hc % 2
                    hc += 1
                    heads.append({
                        "hh": hh, "pp": pp, "z": z,
                        "q_ap": qtb[:, hh, 2 * P:2 * P + 2, :].rearrange(
                            "p j d -> p (j d)"),
                        "out_ap": ab4[:, 2 * P:2 * P + 2, hh, :],
                        "avp": fb[2 + pp][:, 0:256],
                        "bcp": fb[4 + pp][:, 0:256],
                        "rcp": RCP[:, pp, :],
                        "bcs": BCS[:, pp, :],
                        "t_exp": {}, "t_av": None,
                    })

                def emit_st(H, g, ngrp=ngrp):
                    bp = H["z"]
                    pp = H["pp"]
                    tm = None
                    for b2 in range(2):
                        b = 2 * g + b2
                        wt = [stp_war[bp], tk_all] if b2 == 0 else ()
                        tm = E("tensor", lambda e, o=fb[bp][:, b2 * 256:(b2 + 1) * 256],
                               l=ktb[:, b * 128:(b + 1) * 128], r=H["q_ap"],
                               s=(b2 == 0), t=(b2 == 1):
                               e.matmul(o, l, r, start=s, stop=t, skip_group_check=True),
                               waits=wt, inc=(b2 == 1))
                    tmk = tm
                    if g >= ngrp - 2:
                        r0 = 2 * (g - (ngrp - 2))
                        tmk = E("vector", lambda e, o=fb[bp][:, 0:512],
                                m=MSK2[:, r0:r0 + 2, :].rearrange("p a b -> p (a b)"):
                                e.tensor_tensor(out=o, in0=o, in1=m, op=ALU.min),
                                waits=[tm], inc=True)
                    H["t_exp"][g] = E("scalar",
                                      lambda e, o=ptv[:, pp, g % 8, :],
                                      i=fb[bp][:, 0:512]:
                                      e.activation(out=o, in_=i, func=AF.Exp,
                                                   bias=c_negb, scale=1.0),
                                      waits=[tmk, ptw[pp][g % 8]], inc=True)
                    stp_war[bp] = H["t_exp"][g]

                def emit_sums_av(H, g, ngrp=ngrp, Epair=Epair):
                    pp = H["pp"]
                    E("tensor", lambda e, o=fb[4 + pp][:, 0:512],
                      r=ptv[:, pp, g % 8, :],
                      s=(g == 0), t=(g == ngrp - 1):
                      e.matmul(o, OMAT[:, :], r, start=s, stop=t,
                               skip_group_check=True),
                      waits=[H["t_exp"][g]] + ([sm_war[pp]] if g == 0 else []),
                      inc=(g == ngrp - 1))
                    tm = None
                    for b2 in range(2):
                        b = 2 * g + b2
                        wt = [H["t_exp"][g]] + ([av_war[pp]] if b == 0 else [])
                        tm = E("tensor", lambda e, o=H["avp"], l=vtb[:, b, :],
                               r=ptv[:, pp, g % 8, b2 * 256:(b2 + 1) * 256],
                               s=(b == 0), t=(b == Epair - 1):
                               e.matmul(o, l, r, start=s, stop=t, skip_group_check=True),
                               waits=wt, inc=(b == Epair - 1))
                    ptw[pp][g % 8] = tm
                    H["t_av"] = tm

                A, Bh = heads
                emit_st(A, 0)
                emit_st(Bh, 0)
                flush_tail()
                flush_tail()
                flush_mult()
                for g in range(ngrp):
                    if g + 1 < ngrp:
                        emit_st(A, g + 1)
                    emit_sums_av(A, g)
                    if g + 1 < ngrp:
                        emit_st(Bh, g + 1)
                    emit_sums_av(Bh, g)
                for hd in heads:
                    pp = hd["pp"]
                    headq.append((hd["out_ap"], pp, hd["avp"], hd["bcs"], hd["bcp"],
                                  hd["rcp"],
                                  fb[4 + pp][0:1, 0:512].rearrange(
                                      "p (e q) -> p q e", e=2),
                                  (ENGSEM["tensor"], p.cnt["pe"]), hd["t_av"]))
        flush_tail()
        flush_tail()
        flush_mult()
        attn_dma = E("scalar", lambda e,
                     o=attn_scr[:, :, kv * 4:(kv + 1) * 4, :].rearrange(
                         "j p h d -> p j h d"),
                     s=ab4[:, :, :, :]: e.dma_start(out=o, in_=s),
                     waits=[last_attn_tok], dma="sAT")
        s4_pe_last = (ENGSEM["tensor"], p.cnt["pe"])
        if kv == 0:
            wo_pre = []
            for pr in range(2):
                t = None
                for i in range(2):
                    t = E("gpsimd", lambda e, o=wb4[:, i * 16:(i + 1) * 16, pr, :],
                          s=wo_v[:, i * 16:(i + 1) * 16, pr * 512:(pr + 1) * 512]:
                          e.dma_start(out=o, in_=s), dma="gW%d" % pr)
                wo_pre.append(t)
    p.barrier()
    if stop_after == "s4":
        return _finish()

    ab2 = ABUF[:, :].rearrange("p (s h d) -> p s h d", s=2, h=32)
    tb2 = TBUF[:, 0:8192].rearrange("p (s c d) -> p s c d", s=2, c=32)
    at_sb = TBUF[:, 8192:12288].rearrange("p (c r) -> p c r", c=32)
    bcb = TBUF[:, 12288:16384]
    hT = STG[:, :, :].rearrange("p a b -> p (a b)")
    load_weight_cast(a_v, at_sb, 1, "gmisc")
    E("gpsimd", lambda e: e.dma_start(out=bcb, in_=bc_f), dma="gmisc")
    ta_misc = E("gpsimd", lambda e: e.dma_start(out=BIAS[:, :], in_=bias_f),
                dma="gmisc")
    hT_toks = [None] * NSLOT
    s6state = {"slot_cons": [[], []], "pbuf1_cons": None,
               "txn2store": [None, None], "hT_prev": None}

    def s6_body(j, tsq_tok, ystore_tok):
        sl = j % 2
        sc6 = s6state
        ty = E("sync", lambda e, o=XBUF[:, sl, :], s=y_scr[j, :, :]:
               e.dma_start(out=o, in_=s),
               waits=sc6["slot_cons"][sl] + [ystore_tok], dma="lXB%d" % sl)
        sc6["slot_cons"][sl] = []
        tx = E("sync", lambda e, o=XBUF[:, 1 - sl, :], s=x_own[j, :, :]:
               e.dma_start(out=o, in_=s), waits=sc6["slot_cons"][1 - sl],
               dma="lXB%d" % (1 - sl))
        sc6["slot_cons"][1 - sl] = []
        L = LNS[:, sl, :]
        E("vector", lambda e, o=L[:, 0:1], i=LN2S[:, j, :]: e.tensor_reduce(
            out=o, in_=i, axis=AX.X, op=ALU.add), waits=[t_rstd[sl], tsq_tok])
        tcert = E("vector", lambda e, o=L[:, 2:3], i=LN2Q[:, j, :]: e.tensor_reduce(
            out=o, in_=i, axis=AX.X, op=ALU.add), inc=True, drain=True)
        mean, rstd, tr = ln_stats_fin(L[:, 0:1], L[:, 2:3], sl, [tcert])
        ty2 = E("vector", lambda e, o=XBUF[:, sl, :], m=mean, r=rstd: e.tensor_scalar(
            out=o, in0=o, scalar1=m, scalar2=r, op0=ALU.subtract, op1=ALU.mult),
          waits=[tr, ty], inc=True)
        t_lnuse[sl] = ty2
        p.D("vector")
        tx2 = E("vector", lambda e, o=XBUF[:, sl, :], b=XBUF[:, 1 - sl, :]:
                e.scalar_tensor_tensor(out=o, in0=o, scalar=c_one, in1=b,
                                       op0=ALU.mult, op1=ALU.add,
                                       accum_out=LNP[:, 2:3]),
                waits=[tx], inc=True, drain=True)
        sc6["slot_cons"][1 - sl].append(tx2)
        E("scalar", lambda e, o=PBUF[:, 0, :], i=XBUF[:, sl, :]:
          e.activation(out=o, in_=i, func=AF.Square, bias=0.0, scale=1.0,
                       accum_out=LNP[:, 3:4]), waits=[tx2])
        mean3, rstd3, tr3 = ln_stats_fin(LNP[:, 2:3], LNP[:, 3:4], sl, [tx2])
        txn2 = E("vector", lambda e, o=PBUF[:, 1, :], i=XBUF[:, sl, :], m=mean3,
                 r=rstd3: e.tensor_scalar(out=o, in0=i, scalar1=m, scalar2=r,
                                          op0=ALU.subtract, op1=ALU.mult),
                 waits=[tr3, sc6["pbuf1_cons"]], inc=True)
        t_lnuse[sl] = txn2
        sc6["slot_cons"][sl].append(txn2)
        tx2dma = E("scalar", lambda e, o=x2_scr[j, :, :], s=XBUF[:, sl, :]:
                   e.dma_start(out=o, in_=s), waits=[txn2], dma="sX2")
        sc6["slot_cons"][sl].append(tx2dma)
        tpe, tac = transpose_chunks(
            lambda c: PBUF[:, 1, c * 128:(c + 1) * 128],
            lambda c, sl=sl: tb2[:, sl, c, :],
            32, txn2, first_waits=[sc6["txn2store"][sl], sc6.get("hT_%d" % sl)])
        sc6["pbuf1_cons"] = tpe
        sc6["txn2store"][sl] = E("scalar", lambda e, o=xn2T_scr[j, :, :, :],
                                 s=tb2[:, sl, :, :]: e.dma_start(out=o, in_=s),
                                 waits=[tac], dma="sT2%d" % sl)
        bp2 = 2 + sl
        tmm = None
        for dc in range(32):
            w = ()
            if dc == 0:
                w = [tac, ta_misc, sc6["hT_prev"]]
            tmm = E("tensor", lambda e, o=fb[bp2][:, 0:128], l=at_sb[:, dc, :],
                    r=tb2[:, sl, dc, :], s=(dc == 0), t=(dc == 31):
                    e.matmul(o, l, r, start=s, stop=t), waits=w, inc=(dc == 31))
        th = E("vector", lambda e, o=hT[:, j * 128:(j + 1) * 128], i=fb[bp2][:, 0:128]:
               e.tensor_scalar(out=o, in0=i, scalar1=CO[:, 0:1], scalar2=None,
                               op0=ALU.mult), waits=[tmm], inc=True)
        hT_toks[j] = th
        sc6["hT_prev"] = th
        sc6["hT_%d" % sl] = tmm

    w_parts_done = [None, None]
    at_cons = [None, None]
    ev_done = [None, None]
    ystore = [None, None]
    for part in range(8):
        wsl = part % 2
        if part < 2:
            tw = wo_pre[part]
        else:
            tw = None
            for i in range(2):
                tw = E("gpsimd", lambda e, o=wb4[:, i * 16:(i + 1) * 16, wsl, :],
                       s=wo_v[:, i * 16:(i + 1) * 16, part * 512:(part + 1) * 512]:
                       e.dma_start(out=o, in_=s),
                       waits=[w_parts_done[wsl]] if i == 0 else (), dma="gW%d" % wsl)
        for j in range(NSLOT):
            sl = j % 2
            ta = E("sync", lambda e, o=ab2[:, sl, :, :], s=attn_scr[j, :, :, :]:
                   e.dma_start(out=o, in_=s), waits=[at_cons[sl]], dma="lAB%d" % sl)
            bp = j % 2
            tmm = None
            for dc in range(32):
                w = [ta, tw] + (ev_done[bp] or []) if dc == 0 else ()
                tmm = E("tensor", lambda e, o=fb[bp][:, :], l=ab2[:, sl, dc, :],
                        r=wb4[:, dc, wsl, :], s=(dc == 0), t=(dc == 31):
                        e.matmul(o, l, r, start=s, stop=t), waits=w, inc=(dc == 31))
            at_cons[sl] = tmm
            tev = E("vector", lambda e, o=YST[:, sl, :], i=fb[bp][:, :],
                    a=LN2S[:, j, part:part + 1]:
                    e.tensor_scalar(out=o, in0=i, scalar1=0.0, scalar2=None,
                                    op0=ALU.add, op1=ALU.add, accum_out=a),
                    waits=[tmm, ystore[sl]], inc=True)
            tsq = E("scalar", lambda e, o=PBUF[:, 0, 0:512], i=fb[bp][:, :],
                    a=LN2Q[:, j, part:part + 1]:
                    e.activation(out=o, in_=i, func=AF.Square, bias=0.0, scale=1.0,
                                 accum_out=a),
                    waits=[tev], inc=True)
            ev_done[bp] = [tev, tsq]
            ystore[sl] = E("scalar", lambda e, o=y_scr[j, :, part * 512:(part + 1) * 512],
                           s=YST[:, sl, :]: e.dma_start(out=o, in_=s),
                           waits=[tev], dma="sY%d" % sl)
            if part == 7:
                s6_body(j, tsq, ystore[sl])
        w_parts_done[wsl] = (ENGSEM["tensor"], p.cnt["pe"])
    p.barrier()
    if stop_after in ("s5", "s6"):
        return _finish()

    xt2 = TBUF[:, 0:8192].rearrange("p (s c d) -> p s c d", s=2, c=32)

    s8state = {"slot_cons": [[], []]}

    def s8_body(j, tsq_tok, fstore_tok):
        sl = j % 2
        sc8 = s8state
        tf = E("sync", lambda e, o=XBUF[:, sl, :], s=f_scr[j, :, :]:
               e.dma_start(out=o, in_=s), waits=sc8["slot_cons"][sl] + [fstore_tok],
               dma="lXB%d" % sl)
        sc8["slot_cons"][sl] = []
        tx2 = E("sync", lambda e, o=XBUF[:, 1 - sl, :], s=x2_scr[j, :, :]:
                e.dma_start(out=o, in_=s), waits=sc8["slot_cons"][1 - sl],
                dma="lXB%d" % (1 - sl))
        sc8["slot_cons"][1 - sl] = []
        L = LNS[:, sl, :]
        E("vector", lambda e, o=L[:, 0:1], i=LN4S[:, j, :]: e.tensor_reduce(
            out=o, in_=i, axis=AX.X, op=ALU.add), waits=[t_rstd[sl], tsq_tok])
        tcert = E("vector", lambda e, o=L[:, 2:3], i=LN4Q[:, j, :]: e.tensor_reduce(
            out=o, in_=i, axis=AX.X, op=ALU.add), inc=True, drain=True)
        mean, rstd, tr = ln_stats_fin(L[:, 0:1], L[:, 2:3], sl, [tcert])
        ty4 = E("vector", lambda e, o=XBUF[:, sl, :], m=mean, r=rstd: e.tensor_scalar(
            out=o, in0=o, scalar1=m, scalar2=r, op0=ALU.subtract, op1=ALU.mult),
          waits=[tr, tf], inc=True)
        t_lnuse[sl] = ty4
        p.D("vector")
        tout = E("vector", lambda e, o=XBUF[:, sl, :], b=XBUF[:, 1 - sl, :]:
                 e.tensor_tensor(out=o, in0=o, in1=b, op=ALU.add),
                 waits=[tx2], inc=True)
        sc8["slot_cons"][1 - sl].append(tout)
        todma = E("scalar", lambda e, o=out_own[j, :, :], s=XBUF[:, sl, :]:
                  e.dma_start(out=o, in_=s), waits=[tout], dma="sOUT")
        sc8["slot_cons"][sl].append(todma)

    xn2_cons = [None, None]
    w_parts_done = [None, None]
    ev_done = [None, None]
    fstore = [None, None]
    for part in range(8):
        wsl = part % 2
        tw = None
        for i in range(2):
            tw = E("gpsimd", lambda e, o=wb4[:, i * 16:(i + 1) * 16, wsl, :],
                   s=w_v[:, i * 16:(i + 1) * 16, part * 512:(part + 1) * 512]:
                   e.dma_start(out=o, in_=s),
                   waits=[w_parts_done[wsl]] if i == 0 else (), dma="gW%d" % wsl)
        for j in range(NSLOT):
            sl = j % 2
            txt = E("sync", lambda e, o=xt2[:, sl, :, :], s=xn2T_scr[j, :, :, :]:
                    e.dma_start(out=o, in_=s), waits=[xn2_cons[sl]], dma="lTB%d" % sl)
            bp = j % 2
            for dc in range(32):
                w = ()
                if dc == 0:
                    w = [txt, hT_toks[j], ta_misc, tw] + (ev_done[bp] or [])
                E("tensor", lambda e, o=fb[bp][:, :], l=xt2[:, sl, dc, :],
                  r=wb4[:, dc, wsl, :], s=(dc == 0):
                  e.matmul(o, l, r, start=s, stop=False), waits=w)
            E("tensor", lambda e, o=fb[bp][:, :], l=hT[:, j * 128:(j + 1) * 128],
              r=bcb[:, part * 512:(part + 1) * 512]:
              e.matmul(o, l, r, start=False, stop=False, skip_group_check=True))
            tmm = E("tensor", lambda e, o=fb[bp][:, :],
                    r=BIAS[:, part * 512:(part + 1) * 512]:
                    e.matmul(o, ORB[:, :], r, start=False, stop=True,
                             skip_group_check=True), inc=True)
            xn2_cons[sl] = tmm
            tev = E("vector", lambda e, o=YST[:, sl, :], i=fb[bp][:, :],
                    a=LN4S[:, j, part:part + 1]:
                    e.tensor_scalar(out=o, in0=i, scalar1=0.0, scalar2=None,
                                    op0=ALU.add, op1=ALU.add, accum_out=a),
                    waits=[tmm, fstore[sl]], inc=True)
            tsq = E("scalar", lambda e, o=PBUF[:, 0, 0:512], i=fb[bp][:, :],
                    a=LN4Q[:, j, part:part + 1]:
                    e.activation(out=o, in_=i, func=AF.Square, bias=0.0, scale=1.0,
                                 accum_out=a),
                    waits=[tev], inc=True)
            ev_done[bp] = [tev, tsq]
            fstore[sl] = E("scalar", lambda e, o=f_scr[j, :, part * 512:(part + 1) * 512],
                           s=YST[:, sl, :]: e.dma_start(out=o, in_=s),
                           waits=[tev], dma="sY%d" % sl)
            if part == 7:
                s8_body(j, tsq, fstore[sl])
        w_parts_done[wsl] = (ENGSEM["tensor"], p.cnt["pe"])
    return _finish()


def _host_tables():
    pos = np.arange(S, dtype=np.float64)[:, None]
    freqs = np.exp(-np.arange(0, HD, 2, dtype=np.float64) * (math.log(10000.0) / HD))
    ang = pos * freqs
    c = np.cos(ang).T.astype(np.float32)
    s = np.sin(ang).T.astype(np.float32)
    return np.vstack([c, c]), np.vstack([s, s])


def make_in_maps(L0_attn_hidden_states, L0_attn_Wq, L0_attn_Wk, L0_attn_Wv, L0_attn_Wo,
                 L0_ffn_lora_A, L0_ffn_lora_B, L0_ffn_W, L0_ffn_bias,
                 L0_ffn_lora_weights):
    x = np.ascontiguousarray(np.asarray(L0_attn_hidden_states, dtype=np.float32))
    wq = np.ascontiguousarray(np.asarray(L0_attn_Wq, dtype=np.float32).T)
    wk = np.ascontiguousarray(np.asarray(L0_attn_Wk, dtype=np.float32).T)
    wv = np.ascontiguousarray(np.asarray(L0_attn_Wv, dtype=np.float32).T)
    wo = np.ascontiguousarray(np.asarray(L0_attn_Wo, dtype=np.float32).T)
    wff = np.ascontiguousarray(np.asarray(L0_ffn_W, dtype=np.float32).T)
    la = np.asarray(L0_ffn_lora_A, dtype=np.float32)
    lb = np.asarray(L0_ffn_lora_B, dtype=np.float32)
    a_t = np.ascontiguousarray(la.reshape(128, D).T)
    bc = np.ascontiguousarray(lb.transpose(0, 2, 1).reshape(128, D))
    bias = np.ascontiguousarray(np.asarray(L0_ffn_bias, dtype=np.float32).reshape(1, D))
    wrep = np.ascontiguousarray(np.repeat(
        np.asarray(L0_ffn_lora_weights, dtype=np.float32), 16)[:, None])

    cosk, sink = _host_tables()
    sc = 1.0 / math.sqrt(HD)

    ccol = np.zeros((128, 8), dtype=np.float32)
    ccol[:, 0] = 1.0
    ccol[:, 1] = -1.0
    ccol[:, 2] = 2.0
    ccol[:, 3] = 1e-5
    ccol[:, 4] = 1.0 / D
    ccol[:, 5] = -BOUND
    ident = np.eye(128, dtype=ml_dtypes.bfloat16)
    ones_col = np.ones((128, 1), dtype=ml_dtypes.bfloat16)
    ones_row_f = np.ones((1, 128), dtype=np.float32)
    ones_row_bf = np.ones((1, 128), dtype=ml_dtypes.bfloat16)
    ones_mat = np.ones((128, 128), dtype=ml_dtypes.bfloat16)

    kk = np.arange(128)[:, None]
    qq = np.arange(128)[None, :]
    diag = np.where(qq >= kk, 1e9, -1e9).astype(np.float32)
    allv = np.full((128, 128), 1e9, dtype=np.float32)
    allm = np.full((128, 128), -1e9, dtype=np.float32)

    def pair_mask(o):
        m = np.empty((128, 4, 2, 128), dtype=np.float32)
        for rel in range(4):
            for s in range(2):
                d = rel - 2 * s - o
                m[:, rel, s, :] = allv if d < 0 else (diag if d == 0 else allm)
        return np.ascontiguousarray(m.reshape(128, 4 * 256))

    in_maps = []
    for c in range(8):
        b, o = c % 4, c // 4
        tiles = (2 * np.arange(NSLOT) + o) * 128
        idx = (tiles[:, None] + np.arange(128)[None, :]).ravel()
        x_ownc = np.ascontiguousarray(x[b][idx].reshape(NSLOT, 128, D))
        cosq = np.ascontiguousarray(cosk[:, idx] * sc)
        sinq = np.ascontiguousarray(sink[:, idx] * sc)
        mask = pair_mask(o)
        in_maps.append({
            "x_own": x_ownc, "x_kv": x[b],
            "wq_t": wq, "wk_t": wk, "wv_t": wv, "wo_t": wo, "w_t": wff,
            "a_t": a_t, "bc_f": bc, "bias_f": bias, "wrep": wrep,
            "cosk_t": cosk, "sink_t": sink, "cosq_t": cosq, "sinq_t": sinq,
            "maskmin": np.ascontiguousarray(mask), "ccol": ccol,
            "ident": ident, "ones_col": ones_col, "ones_mat": ones_mat,
            "ones_row_f": ones_row_f, "ones_row_bf": ones_row_bf,
        })
    return in_maps


def kernel(**inputs):
    in_maps = make_in_maps(**inputs)
    nc = build_program()
    res = run_bass_kernel_spmd(nc, in_maps, core_ids=list(range(8)))

    out = np.empty((B, S, D), dtype=np.float32)
    for c in range(8):
        b, o = c % 4, c // 4
        oo = res.results[c]["out_own"]
        for j in range(NSLOT):
            t0 = (2 * j + o) * 128
            out[b, t0:t0 + 128, :] = oo[j]
    return out


# version 17
# speedup vs baseline: 1.0849x; 1.0493x over previous
import math
from contextlib import ExitStack

import numpy as np
import ml_dtypes

import concourse.bass as bass
import concourse.mybir as mybir
from concourse.bass_utils import run_bass_kernel_spmd

F32 = mybir.dt.float32
BF16 = mybir.dt.bfloat16
AF = mybir.ActivationFunctionType
ALU = mybir.AluOpType
AX = mybir.AxisListType

B, S, D = 4, 4096, 4096
H, KV, HD = 32, 8, 128
NSLOT = 16
BOUND = 40.0

ENGS = ("sync", "gpsimd", "scalar", "vector", "tensor")
ENGSEM = {"tensor": "pe", "vector": "dv", "scalar": "ac", "gpsimd": "gp", "sync": None}
QSEM = {"sync": "qL", "gpsimd": "qG", "scalar": "qS"}
SEMS = ("qL", "qG", "qS", "pe", "dv", "ac", "gp")


class Plan:

    def __init__(self):
        self.streams = {e: [] for e in ENGS}
        self.cnt = {}
        self.waited = {e: {} for e in ENGS}
        self.pending = {e: [] for e in ENGS}
        self.sem_owner = {}
        for s in ("pe", "dv", "ac", "gp", "sy"):
            self.cnt[s] = 0

    def E(self, eng, fn, waits=(), inc=False, dma=None, drain=False):
        wl = list(self.pending[eng])
        self.pending[eng] = []
        wl += [w for w in waits if w is not None]
        eff = []
        own = ENGSEM[eng]
        for (s, v) in wl:
            if s == own:
                continue
            if v <= self.waited[eng].get(s, 0):
                continue
            self.waited[eng][s] = v
            eff.append((s, v))
        incs = []
        tok = None
        if dma is not None:
            assert eng in QSEM, eng
            self.sem_owner.setdefault(dma, eng)
            assert self.sem_owner[dma] == eng, (dma, eng)
            self.cnt[dma] = self.cnt.get(dma, 0) + 16
            incs.append((dma, 16))
            tok = (dma, self.cnt[dma])
        elif inc and not drain:
            self.cnt[own] += 1
            incs.append((own, 1))
            tok = (own, self.cnt[own])
        self.streams[eng].append([eff, fn, incs])
        if drain:
            dincs = []
            if inc:
                self.cnt[own] += 1
                dincs.append((own, 1))
                tok = (own, self.cnt[own])
            self.streams[eng].append([[], lambda e: e.drain(), dincs])
        return tok

    def D(self, eng):
        self.streams[eng].append([[], lambda e: e.drain(), []])

    def barrier(self):
        toks = []
        for e in ENGS:
            st = self.streams[e]
            if not st:
                continue
            eff, fn, incs = st[-1]
            own = ENGSEM[e] or "gp"
            if incs:
                toks.append((incs[-1][0], self.cnt[incs[-1][0]]))
            else:
                self.cnt[own] += 1
                incs.append((own, 1))
                toks.append((own, self.cnt[own]))
            if e in QSEM:
                drains = [(s, self.cnt[s]) for s, o in self.sem_owner.items()
                          if o == e and self.cnt[s] > self.waited[e].get(s, 0)]
                if drains:
                    self.E(e, lambda en: en.nop(), waits=drains)
                    own2 = {"sync": "sy", "gpsimd": "gp", "scalar": "ac"}[e]
                    self.cnt[own2] = self.cnt.get(own2, 0) + 1
                    self.streams[e][-1][2].append((own2, 1))
                    toks.append((own2, self.cnt[own2]))
        for e2 in ENGS:
            self.pending[e2] += toks


def ceil_div(a, b):
    return -(-a // b)


def build_program(dbg=False, stop_after="s8"):
    nc = bass.Bass("TRN2", target_bir_lowering=False)

    def inp(name, shape, dt=F32):
        return nc.dram_tensor(name, shape, dt, kind="ExternalInput").ap()

    x_own = inp("x_own", [NSLOT, 128, D])
    x_kv = inp("x_kv", [S, D])
    wq_t = inp("wq_t", [D, D])
    wk_t = inp("wk_t", [D, KV * HD])
    wv_t = inp("wv_t", [D, KV * HD])
    wo_t = inp("wo_t", [D, D])
    w_t = inp("w_t", [D, D])
    a_t = inp("a_t", [D, 128])
    bc_f = inp("bc_f", [128, D])
    bias_f = inp("bias_f", [1, D])
    wrep = inp("wrep", [128, 1])
    cosk_t = inp("cosk_t", [128, S])
    sink_t = inp("sink_t", [128, S])
    cosq_t = inp("cosq_t", [128, NSLOT * 128])
    sinq_t = inp("sinq_t", [128, NSLOT * 128])
    maskmin = inp("maskmin", [128, 4 * 256])
    ccol = inp("ccol", [128, 8])
    ident_i = inp("ident", [128, 128], BF16)
    ones_col_i = inp("ones_col", [128, 1], BF16)
    ones_row_f_i = inp("ones_row_f", [1, 128])
    ones_row_bf_i = inp("ones_row_bf", [1, 128], BF16)
    ones_mat_i = inp("ones_mat", [128, 128], BF16)

    out_own = nc.dram_tensor("out_own", [NSLOT, 128, D], F32, kind="ExternalOutput").ap()

    def scr(name, shape, dt):
        kind = "ExternalOutput" if dbg else "Internal"
        return nc.dram_tensor(name, shape, dt, kind=kind).ap()

    xnT_scr = scr("xnT_scr", [8, 128, 32, 512], BF16)
    xnTo_scr = scr("xnTo_scr", [4, 128, 32, 512], BF16)
    K_scr = scr("K_scr", [KV, 128, S], BF16)
    V_scr = scr("V_scr", [KV, 128, 32, 128], BF16)
    Q_scr = scr("Q_scr", [KV, 128, 4, NSLOT, 128], BF16)
    attn_scr = scr("attn_scr", [NSLOT, 128, H, 128], BF16)
    y_scr = scr("y_scr", [NSLOT, 128, D], F32)
    x2_scr = scr("x2_scr", [NSLOT, 128, D], F32)
    xn2T_scr = scr("xn2T_scr", [NSLOT, 128, 32, 128], BF16)
    f_scr = scr("f_scr", [NSLOT, 128, D], F32)
    if dbg and dbg >= 2:
        dbg_sums = scr("dbg_sums", [KV, NSLOT, 4, 128], F32)
        dbg_rcp = scr("dbg_rcp", [KV, NSLOT, 4, 128], F32)
        dbg_pt = scr("dbg_pt", [NSLOT, 128, D], BF16)

    al = nc.alloc_sbuf_tensor
    WBUF = al("WBUF", [128, 32, 1024], BF16)
    TBUF = al("TBUF", [128, 16384], BF16)
    XBUF = al("XBUF", [128, 2, 4096], F32)
    PBUF = al("PBUF", [128, 2, 4096], BF16)
    ABUF = al("ABUF", [128, 8192], BF16)
    STG = al("STG", [128, 2, 1024], BF16)
    RBUF = al("RBUF", [128, 2, 2, 512], F32)
    RTMP = al("RTMP", [128, 2, 512], F32)
    MSK2 = al("MSK2", [128, 4, 256], F32)
    LNS = al("LNS", [128, 2, 8], F32)
    LNP = al("LNP", [128, 8], F32)
    LN2S = al("LN2S", [128, NSLOT, 8], F32)
    LN2Q = al("LN2Q", [128, NSLOT, 8], F32)
    LN4S = al("LN4S", [128, NSLOT, 8], F32)
    LN4Q = al("LN4Q", [128, NSLOT, 8], F32)
    CC = al("CC", [128, 8], F32)
    CO = al("CO", [128, 1], F32)
    WREP = al("WREP", [128, 1], F32)
    IDT = al("IDT", [128, 128], BF16)
    OCOL = al("OCOL", [128, 1], BF16)
    ORF = al("ORF", [1, 128], F32)
    ORB = al("ORB", [1, 128], BF16)
    BIAS = al("BIAS", [1, D], BF16)
    SUMS = al("SUMS", [1, 256], F32)
    RCP = al("RCP", [1, 2, 256], BF16)
    OMAT = al("OMAT", [128, 128], BF16)
    YST = al("YST", [128, 2, 512], F32)
    BCS = al("BCS", [128, 2, 256], F32)

    c_one = CC[:, 0:1]
    c_two = CC[:, 2:3]
    c_eps = CC[:, 3:4]
    c_invdm = CC[:, 4:5]
    c_negb = CC[:, 5:6]
    nc.const_aps.register(F32, 1e-5, c_eps)

    fb = [nc.alloc_psum_tensor(f"fb{i}", [128, 512], F32) for i in range(6)]
    tb_ps = [nc.alloc_psum_tensor(f"tp{i}", [128, 128], BF16) for i in range(2)]

    p = Plan()
    E = p.E

    t_rstd = [None, None]
    t_lnuse = [None, None]

    def ln_stats_fin(sum_ap, ssq_ap, par, waits):
        L = LNS[:, par, :]
        E("scalar", lambda e, o=L[:, 1:2], i=sum_ap: e.activation(
            out=o, in_=i, func=AF.Copy, bias=0.0, scale=c_invdm),
          waits=list(waits) + [t_lnuse[par]])
        E("scalar", lambda e, o=L[:, 0:1], i=ssq_ap: e.activation(
            out=o, in_=i, func=AF.Copy, bias=0.0, scale=c_invdm))
        E("scalar", lambda e, o=L[:, 3:4], i=L[:, 1:2]: e.activation(
            out=o, in_=i, func=AF.Square, bias=0.0, scale=1.0))
        E("scalar", lambda e, o=L[:, 4:5], i=L[:, 3:4]: e.activation(
            out=o, in_=i, func=AF.Copy, bias=1e-5, scale=-1.0))
        E("scalar", lambda e, o=L[:, 6:7], i=L[:, 4:5]: e.activation(
            out=o, in_=i, func=AF.Copy, bias=0.0, scale=1.0))
        E("scalar", lambda e, o=L[:, 5:6], i=L[:, 6:7]: e.add(
            out=o, in_=i, add=L[:, 0:1]))
        E("scalar", lambda e, o=L[:, 6:7], i=L[:, 5:6]: e.activation(
            out=o, in_=i, func=AF.Ln, bias=0.0, scale=1.0))
        tr = E("scalar", lambda e, o=L[:, 7:8], i=L[:, 6:7]: e.activation(
            out=o, in_=i, func=AF.Exp, bias=0.0, scale=-0.5),
            inc=True, drain=True)
        t_rstd[par] = tr
        return L[:, 1:2], L[:, 7:8], tr

    def ln_full(x_ap, xn_ap, scrap_ap, par, waits, scrap_war=None):
        L = LNS[:, par, :]
        tcert = E("vector", lambda e, o=L[:, 0:1], i=x_ap: e.tensor_reduce(
            out=o, in_=i, axis=AX.X, op=ALU.add),
          waits=list(waits) + [t_rstd[par]], inc=True, drain=True)
        E("scalar", lambda e, o=scrap_ap, i=x_ap, a=L[:, 2:3]:
          e.activation(out=o, in_=i, func=AF.Square, bias=0.0, scale=1.0,
                       accum_out=a), waits=list(waits))
        mean, rstd, tr = ln_stats_fin(L[:, 0:1], L[:, 2:3], par, [tcert])
        txn = E("vector", lambda e, o=xn_ap, i=x_ap, m=mean, r=rstd: e.tensor_scalar(
            out=o, in0=i, scalar1=m, scalar2=r, op0=ALU.subtract, op1=ALU.mult),
          waits=[tr], inc=True)
        t_lnuse[par] = txn
        return txn

    tr_war = [None, None]

    def transpose_chunks(src_ap_fn, dst_ap_fn, n, t_src, first_waits=()):
        tpe = tac = None
        for c in range(n):
            par = c % 2
            w = [t_src, tr_war[par]]
            if c == 0:
                w += list(first_waits)
            tpe = E("tensor", lambda e, o=tb_ps[par][:, :], i=src_ap_fn(c): e.transpose(
                o, i, IDT[:, :]), waits=w, inc=True)
            tac = E("scalar", lambda e, o=dst_ap_fn(c), i=tb_ps[par][:, :]: e.copy(
                out=o, in_=i), waits=[tpe], inc=True)
            tr_war[par] = tac
        return tpe, tac

    def rope5(ps_ap, cos_ap, sin_ap, out_ap, waits):
        t1 = RTMP[:, 0, :]
        u = RTMP[:, 1, :]
        E("vector", lambda e, o=t1, a=ps_ap, b=cos_ap: e.tensor_tensor(
            out=o, in0=a, in1=b, op=ALU.mult), waits=waits)
        E("vector", lambda e, o=u[0:64, :], a=ps_ap[64:128, :], b=sin_ap[0:64, :]:
          e.tensor_tensor(out=o, in0=a, in1=b, op=ALU.mult))
        E("vector", lambda e, o=u[64:128, :], a=ps_ap[0:64, :], b=sin_ap[64:128, :]:
          e.tensor_tensor(out=o, in0=a, in1=b, op=ALU.mult))
        E("vector", lambda e, o=out_ap[0:64, :], a=t1[0:64, :], b=u[0:64, :]:
          e.tensor_tensor(out=o, in0=a, in1=b, op=ALU.subtract))
        tk = E("vector", lambda e, o=out_ap[64:128, :], a=t1[64:128, :], b=u[64:128, :]:
               e.tensor_tensor(out=o, in0=a, in1=b, op=ALU.add), inc=True)
        return tk

    def load_weight_cast(dram_ap_3d, dst_ap_3d, nsplit, sem, waits=(),
                         per_split=False):
        C = dram_ap_3d.shape[1]
        step = C // nsplit
        toks = []
        for i in range(nsplit):
            w = list(waits) if i == 0 else ()
            sm = sem[i] if per_split else sem
            toks.append(E("gpsimd",
                          lambda e, o=dst_ap_3d[:, i * step:(i + 1) * step, :],
                          s=dram_ap_3d[:, i * step:(i + 1) * step, :]:
                          e.dma_start(out=o, in_=s), waits=w, dma=sm))
        return toks if per_split else toks[-1]

    def _finish():
        p.barrier()
        E("gpsimd", lambda e: e.nop())
        n_ins = {e: len(p.streams[e]) for e in ENGS}
        print("kernel build: instructions per engine:", n_ins,
              "total", sum(n_ins.values()),
              "sbuf_remaining", nc.sbuf_bytes_remaining,
              "psum_banks_remaining", nc.psum_banks_remaining)
        with ExitStack() as ctx:
            sems = {n: ctx.enter_context(nc.semaphore(n)) for n in p.cnt}
            block = ctx.enter_context(nc.Block())

            def make_replayer(stream):
                def f(eng):
                    for eff, fn, incs in stream:
                        for (s, v) in eff:
                            eng.wait_ge(sems[s], v)
                        ins = fn(eng)
                        for (s, a) in incs:
                            ins.then_inc(sems[s], a)
                return f

            block.sync(make_replayer(p.streams["sync"]))
            block.gpsimd(make_replayer(p.streams["gpsimd"]))
            block.scalar(make_replayer(p.streams["scalar"]))
            block.vector(make_replayer(p.streams["vector"]))
            block.tensor(make_replayer(p.streams["tensor"]))
        return nc

    E("sync", lambda e: e.dma_start(out=CC[:, :], in_=ccol), dma="lpre")
    E("sync", lambda e: e.dma_start(out=IDT[:, :], in_=ident_i), dma="lpre")
    E("sync", lambda e: e.dma_start(out=OCOL[:, :], in_=ones_col_i), dma="lpre")
    E("sync", lambda e: e.dma_start(out=ORF[:, :], in_=ones_row_f_i), dma="lpre")
    E("sync", lambda e: e.dma_start(out=ORB[:, :], in_=ones_row_bf_i), dma="lpre")
    E("sync", lambda e: e.dma_start(out=MSK2[:, :, :], in_=maskmin.rearrange(
        "p (a b) -> p a b", a=4)), dma="lpre")
    E("sync", lambda e: e.dma_start(out=OMAT[:, :], in_=ones_mat_i), dma="lpre")
    tpre = E("sync", lambda e: e.dma_start(out=WREP[:, :], in_=wrep), dma="lpre")
    tcc = tpre
    E("vector", lambda e: e.tensor_tensor(out=CO[:, :], in0=WREP[:, :], in1=c_two,
                                          op=ALU.mult), waits=[tpre])

    wk_v = wk_t.rearrange("(c p) m -> p c m", p=128)
    wv_v = wv_t.rearrange("(c p) m -> p c m", p=128)
    wq_v = wq_t.rearrange("(c p) m -> p c m", p=128)
    wo_v = wo_t.rearrange("(c p) m -> p c m", p=128)
    w_v = w_t.rearrange("(c p) m -> p c m", p=128)
    a_v = a_t.rearrange("(c p) m -> p c m", p=128)

    wb3 = WBUF[:, :, :]
    wb4 = WBUF[:, :, :].rearrange("p c (s m) -> p c s m", s=2)
    tb32 = TBUF[:, :].rearrange("p (c t) -> p c t", c=32)

    tw = load_weight_cast(wk_v, wb3, 4, "gW0")
    x_cons = [None, None]
    pb_cons = [None, None]
    rope_tok = [None, None]
    kstore = [None, None]
    rbuf_war = [None, None]
    scrap_tok = None
    tbuf_store = None

    for st in range(8):
        rp = st % 2
        E("sync", lambda e, o=RBUF[:, rp, 0, :],
          s=cosk_t[:, st * 512:(st + 1) * 512]: e.dma_start(out=o, in_=s),
          waits=[rbuf_war[rp]], dma="lRB%d" % rp)
        rbuf_tok = E("sync", lambda e, o=RBUF[:, rp, 1, :],
                     s=sink_t[:, st * 512:(st + 1) * 512]: e.dma_start(out=o, in_=s),
                     dma="lRB%d" % rp)
        ac_last = None
        for q in range(4):
            ti = st * 4 + q
            sl = q % 2
            tx = E("sync", lambda e, o=XBUF[:, sl, :],
                   s=x_kv[ti * 128:(ti + 1) * 128, :]: e.dma_start(out=o, in_=s),
                   waits=[x_cons[sl], tcc], dma="lXB%d" % sl)
            txn = ln_full(XBUF[:, sl, :], PBUF[:, sl, :], ABUF[:, 0:4096], sl,
                          waits=[tx], scrap_war=scrap_tok)
            scrap_tok = txn
            x_cons[sl] = txn
            tpe, tac = transpose_chunks(
                lambda c, sl=sl: PBUF[:, sl, c * 128:(c + 1) * 128],
                lambda c, q=q: tb32[:, c, q * 128:(q + 1) * 128],
                32, txn, first_waits=[pb_cons[sl], tbuf_store])
            pb_cons[sl] = tpe
            ac_last = tac
        tbuf_store = E("scalar", lambda e, o=xnT_scr[st, :, :, :], s=tb32[:, :, :]:
                       e.dma_start(out=o, in_=s), waits=[ac_last], dma="sTB")
        for c in range(KV):
            bp = c % 2
            tmm = None
            for dc in range(32):
                w = ()
                if dc == 0:
                    w = [ac_last, rope_tok[bp]] + ([tw] if (st == 0 and c == 0) else [])
                tmm = E("tensor", lambda e, o=fb[bp][:, :],
                        l=wb3[:, dc, c * 128:(c + 1) * 128], r=tb32[:, dc, :],
                        s=(dc == 0), t=(dc == 31):
                        e.matmul(o, l, r, start=s, stop=t),
                        waits=w, inc=(dc == 31))
            trp = rope5(fb[bp][:, :], RBUF[:, rp, 0, :], RBUF[:, rp, 1, :],
                        STG[:, bp, 0:512], waits=[tmm, rbuf_tok, kstore[bp]])
            rope_tok[bp] = trp
            kstore[bp] = E("scalar", lambda e, o=K_scr[c, :, st * 512:(st + 1) * 512],
                           s=STG[:, bp, 0:512]: e.dma_start(out=o, in_=s),
                           waits=[trp], dma="sST%d" % bp)
        rbuf_war[rp] = (ENGSEM["vector"], p.cnt["dv"])
    p.barrier()
    if stop_after == "s1":
        return _finish()

    ab_stage = ABUF[:, :].rearrange("p (s c d) -> p s c d", s=2, c=32)
    tws = load_weight_cast(wv_v, wb3, 4, ["gW0", "gW1", "gmisc", "gW2"],
                           per_split=True)
    ev_tok = [None, None]
    vstore = [None, None]
    prev_mm = None
    x_cons = [None, None]
    pb_cons = [None, None]
    grp_st = [None, None]
    tx3 = [None, None]

    def s3a_ln(jj):
        sl = jj % 2
        txn = ln_full(XBUF[:, sl, :], PBUF[:, sl, :], PBUF[:, sl, :], sl,
                      waits=[tx3[sl], pb_cons[sl]])
        x_cons[sl] = txn
        return txn

    def s3a_tr(jj, txn):
        g, qq = jj // 4, jj % 4
        sl = jj % 2
        tpe, tac = transpose_chunks(
            lambda c, sl=sl: PBUF[:, sl, c * 128:(c + 1) * 128],
            lambda c, sl=sl: ab_stage[:, sl, c, :],
            32, txn, first_waits=[pb_cons[sl], grp_st[sl]])
        pb_cons[sl] = tpe
        grp_st[sl] = E("scalar",
                       lambda e, o=xnTo_scr[g, :, :, qq * 128:(qq + 1) * 128],
                       s=ab_stage[:, sl, :, :]: e.dma_start(out=o, in_=s),
                       waits=[tac], dma="sA%d" % sl)

    for st in range(8):
        tb_tok = E("sync", lambda e, o=tb32[:, :, :], s=xnT_scr[st, :, :, :]:
                   e.dma_start(out=o, in_=s), waits=[prev_mm], dma="lTB0")
        for dd in range(2):
            jj = 2 * st + dd
            sx = jj % 2
            tx3[sx] = E("sync", lambda e, o=XBUF[:, sx, :], s=x_own[jj, :, :]:
                        e.dma_start(out=o, in_=s), waits=[x_cons[sx]],
                        dma="lXB%d" % sx)
        txn3 = [s3a_ln(2 * st), None]
        if st == 0:
            txn3[1] = s3a_ln(1)
            s3a_tr(0, txn3[0])
            s3a_tr(1, txn3[1])
        for q in range(4):
            ti = st * 4 + q
            sl = q % 2
            for half in range(2):
                bp = half
                tmm = None
                for dc in range(32):
                    w = ()
                    if dc == 0:
                        w = [tb_tok, ev_tok[bp]]
                    if st == 0 and q == 0 and half == 0 and dc % 8 == 0:
                        w = list(w) + [tws[dc // 8]]
                    tmm = E("tensor", lambda e, o=fb[bp][:, :],
                            l=tb32[:, dc, q * 128:(q + 1) * 128],
                            r=wb3[:, dc, half * 512:(half + 1) * 512],
                            s=(dc == 0), t=(dc == 31):
                            e.matmul(o, l, r, start=s, stop=t),
                            waits=w, inc=(dc == 31))
                prev_mm = tmm
                ev_tok[bp] = E("scalar",
                               lambda e, o=STG[:, sl, half * 512:(half + 1) * 512],
                               i=fb[bp][:, :]: e.copy(out=o, in_=i),
                               waits=[tmm, vstore[sl]], inc=True)
            for kv in range(KV):
                vstore[sl] = E("scalar", lambda e, o=V_scr[kv, :, ti, :],
                               s=STG[:, sl, kv * 128:(kv + 1) * 128]:
                               e.dma_start(out=o, in_=s),
                               waits=[ev_tok[0], ev_tok[1]], dma="sST%d" % sl)
            if q == 1 and st > 0:
                txn3[1] = s3a_ln(2 * st + 1)
        if st > 0:
            s3a_tr(2 * st, txn3[0])
            s3a_tr(2 * st + 1, txn3[1])

    def wq_half_load(part, half, waits):
        return load_weight_cast(
            wq_v[:, :, part * 1024 + half * 512:part * 1024 + (half + 1) * 512],
            wb4[:, :, half, :], 2, "gW%d" % half, waits=waits)

    half_ld = [wq_half_load(0, h, waits=[prev_mm]) for h in range(2)]
    p.barrier()
    if stop_after in ("s2", "s3a"):
        return _finish()

    rope_tok = [None, None]
    qstore = [None, None]
    qproj_done = None
    rbuf_war = [None, None]
    half_rd = [None, None]
    for part in range(4):
        prev_g_mm = None
        for g in range(4):
            tb_tok = E("sync", lambda e, o=tb32[:, :, :], s=xnTo_scr[g, :, :, :]:
                       e.dma_start(out=o, in_=s), waits=[qproj_done, prev_g_mm],
                       dma="lTB0")
            rp = g % 2
            E("sync", lambda e, o=RBUF[:, rp, 0, :],
              s=cosq_t[:, g * 512:(g + 1) * 512]: e.dma_start(out=o, in_=s),
              waits=[rbuf_war[rp]], dma="lRB%d" % rp)
            rbuf_tok = E("sync", lambda e, o=RBUF[:, rp, 1, :],
                         s=sinq_t[:, g * 512:(g + 1) * 512]: e.dma_start(out=o, in_=s),
                         dma="lRB%d" % rp)
            for hh in range(8):
                h = part * 8 + hh
                kv, hq = h // 4, h % 4
                hf = hh // 4
                bp = hh % 2
                tmm = None
                for dc in range(32):
                    w = ()
                    if dc == 0:
                        w = [tb_tok, rope_tok[bp]] + (
                            [half_ld[hf]] if (g == 0 and hh % 4 == 0) else [])
                    tmm = E("tensor", lambda e, o=fb[bp][:, :],
                            l=wb4[:, dc, hf, (hh % 4) * 128:(hh % 4 + 1) * 128],
                            r=tb32[:, dc, :],
                            s=(dc == 0), t=(dc == 31):
                            e.matmul(o, l, r, start=s, stop=t),
                            waits=w, inc=(dc == 31))
                prev_g_mm = tmm
                if g == 3 and hh % 4 == 3:
                    half_rd[hf] = tmm
                trp = rope5(fb[bp][:, :], RBUF[:, rp, 0, :], RBUF[:, rp, 1, :],
                            STG[:, bp, 0:512], waits=[tmm, rbuf_tok, qstore[bp]])
                rope_tok[bp] = trp
                qstore[bp] = E("scalar",
                               lambda e, o=Q_scr[kv, :, hq, 4 * g:4 * (g + 1), :],
                               s=STG[:, bp, 0:512].rearrange("p (j d) -> p j d", j=4):
                               e.dma_start(out=o, in_=s), waits=[trp],
                               dma="sST%d" % bp)
            rbuf_war[rp] = (ENGSEM["vector"], p.cnt["dv"])
        qproj_done = (ENGSEM["tensor"], p.cnt["pe"])
        if part < 3:
            half_ld = [wq_half_load(part + 1, h, waits=[half_rd[h]])
                       for h in range(2)]
    p.barrier()
    if stop_after == "s3b":
        return _finish()

    ktb = TBUF[:, 0:4096]
    vtb = TBUF[:, 4096:8192].rearrange("p (t d) -> p t d", t=32)
    qtb = TBUF[:, 8192:16384].rearrange("p (h j d) -> p h j d", h=4, j=NSLOT)
    ab4 = ABUF[:, :].rearrange("p (j h d) -> p j h d", j=NSLOT, h=4)
    ptv = PBUF[:, :, :].rearrange("p s (g q) -> p s g q", g=8)

    attn_dma = None
    s4_pe_last = None
    ptw = [[None] * 8, [None] * 8]
    stp_war = [None, None]
    av_war = [None, None]
    sm_war = [None, None]
    bc_tok = [None, None]
    hc = 0
    last_attn_tok = None
    pend = None
    headq = []

    def flush_mult():
        nonlocal pend, last_attn_tok
        if pend is None:
            return False
        out0, pp0, avp0, bcs0, tbcc0, tav0 = pend
        tok = E("vector", lambda e, o=out0, a=avp0, b=bcs0:
                e.scalar_tensor_tensor(out=o, in0=a, scalar=c_one, in1=b,
                                       op0=ALU.mult, op1=ALU.mult),
                waits=[tbcc0, tav0, attn_dma], inc=True)
        av_war[pp0] = tok
        last_attn_tok = tok
        pend = None
        return True

    def flush_tail():
        nonlocal pend
        if not headq:
            return
        out0, pp0, avp0, bcs0, bcp0, rcp0, sums_row0, t_summ0, t_av0 = headq.pop(0)
        E("vector", lambda e, o=SUMS[:, :], i=sums_row0:
          e.tensor_reduce(out=o, in_=i, axis=AX.X, op=ALU.add),
          waits=[t_summ0])
        if not flush_mult():
            p.D("vector")
        def _recip(e, o=rcp0):
            with nc.allow_low_precision(reason="bf16 softmax-normalizer broadcast"):
                return e.reciprocal(out=o, in_=SUMS[:, :])
        trc = E("vector", _recip, waits=[bc_tok[pp0]], inc=True)
        tbc = E("tensor", lambda e, o=bcp0, r=rcp0: e.matmul(
            o, ORB[:, :], r, start=True, stop=True), waits=[trc], inc=True)
        bc_tok[pp0] = tbc
        tbcc = E("scalar", lambda e, o=bcs0, i=bcp0: e.copy(out=o, in_=i),
                 waits=[tbc, av_war[pp0]], inc=True)
        pend = (out0, pp0, avp0, bcs0, tbcc, t_av0)
        sm_war[pp0] = tbcc

    for kv in range(KV):
        E("sync", lambda e, o=ktb, s=K_scr[kv, :, :]: e.dma_start(out=o, in_=s),
          waits=[s4_pe_last], dma="lTB0")
        E("sync", lambda e, o=vtb, s=V_scr[kv, :, :, :]: e.dma_start(out=o, in_=s),
          dma="lTB0")
        tk_all = E("sync", lambda e, o=qtb, s=Q_scr[kv, :, :, :, :]:
                   e.dma_start(out=o, in_=s), dma="lTB0")
        for P in range(8):
            Epair = 4 * P + 4
            ngrp = Epair // 2
            for hp in range(2):
                heads = []
                for z in range(2):
                    hh = hp * 2 + z
                    pp = hc % 2
                    hc += 1
                    heads.append({
                        "hh": hh, "pp": pp, "z": z,
                        "q_ap": qtb[:, hh, 2 * P:2 * P + 2, :].rearrange(
                            "p j d -> p (j d)"),
                        "out_ap": ab4[:, 2 * P:2 * P + 2, hh, :],
                        "avp": fb[2 + pp][:, 0:256],
                        "bcp": fb[4 + pp][:, 0:256],
                        "rcp": RCP[:, pp, :],
                        "bcs": BCS[:, pp, :],
                        "t_exp": {}, "t_av": None,
                    })

                def emit_st(H, g, ngrp=ngrp):
                    bp = H["z"]
                    pp = H["pp"]
                    tm = None
                    for b2 in range(2):
                        b = 2 * g + b2
                        wt = [stp_war[bp], tk_all] if b2 == 0 else ()
                        tm = E("tensor", lambda e, o=fb[bp][:, b2 * 256:(b2 + 1) * 256],
                               l=ktb[:, b * 128:(b + 1) * 128], r=H["q_ap"],
                               s=(b2 == 0), t=(b2 == 1):
                               e.matmul(o, l, r, start=s, stop=t, skip_group_check=True),
                               waits=wt, inc=(b2 == 1))
                    tmk = tm
                    if g >= ngrp - 2:
                        r0 = 2 * (g - (ngrp - 2))
                        tmk = E("vector", lambda e, o=fb[bp][:, 0:512],
                                m=MSK2[:, r0:r0 + 2, :].rearrange("p a b -> p (a b)"):
                                e.tensor_tensor(out=o, in0=o, in1=m, op=ALU.min),
                                waits=[tm], inc=True)
                    H["t_exp"][g] = E("scalar",
                                      lambda e, o=ptv[:, pp, g % 8, :],
                                      i=fb[bp][:, 0:512]:
                                      e.activation(out=o, in_=i, func=AF.Exp,
                                                   bias=c_negb, scale=1.0),
                                      waits=[tmk, ptw[pp][g % 8]], inc=True)
                    stp_war[bp] = H["t_exp"][g]

                def emit_sums_av(H, g, ngrp=ngrp, Epair=Epair):
                    pp = H["pp"]
                    E("tensor", lambda e, o=fb[4 + pp][:, 0:512],
                      r=ptv[:, pp, g % 8, :],
                      s=(g == 0), t=(g == ngrp - 1):
                      e.matmul(o, OMAT[:, :], r, start=s, stop=t,
                               skip_group_check=True),
                      waits=[H["t_exp"][g]] + ([sm_war[pp]] if g == 0 else []),
                      inc=(g == ngrp - 1))
                    tm = None
                    for b2 in range(2):
                        b = 2 * g + b2
                        wt = [H["t_exp"][g]] + ([av_war[pp]] if b == 0 else [])
                        tm = E("tensor", lambda e, o=H["avp"], l=vtb[:, b, :],
                               r=ptv[:, pp, g % 8, b2 * 256:(b2 + 1) * 256],
                               s=(b == 0), t=(b == Epair - 1):
                               e.matmul(o, l, r, start=s, stop=t, skip_group_check=True),
                               waits=wt, inc=(b == Epair - 1))
                    ptw[pp][g % 8] = tm
                    H["t_av"] = tm

                A, Bh = heads
                emit_st(A, 0)
                emit_st(Bh, 0)
                flush_tail()
                flush_tail()
                flush_mult()
                for g in range(ngrp):
                    if g + 1 < ngrp:
                        emit_st(A, g + 1)
                    emit_sums_av(A, g)
                    if g + 1 < ngrp:
                        emit_st(Bh, g + 1)
                    emit_sums_av(Bh, g)
                for hd in heads:
                    pp = hd["pp"]
                    headq.append((hd["out_ap"], pp, hd["avp"], hd["bcs"], hd["bcp"],
                                  hd["rcp"],
                                  fb[4 + pp][0:1, 0:512].rearrange(
                                      "p (e q) -> p q e", e=2),
                                  (ENGSEM["tensor"], p.cnt["pe"]), hd["t_av"]))
        flush_tail()
        flush_tail()
        flush_mult()
        attn_dma = E("scalar", lambda e,
                     o=attn_scr[:, :, kv * 4:(kv + 1) * 4, :].rearrange(
                         "j p h d -> p j h d"),
                     s=ab4[:, :, :, :]: e.dma_start(out=o, in_=s),
                     waits=[last_attn_tok], dma="sAT")
        s4_pe_last = (ENGSEM["tensor"], p.cnt["pe"])
        if kv == 0:
            wo_pre = []
            for pr in range(2):
                t = None
                for i in range(2):
                    t = E("gpsimd", lambda e, o=wb4[:, i * 16:(i + 1) * 16, pr, :],
                          s=wo_v[:, i * 16:(i + 1) * 16, pr * 512:(pr + 1) * 512]:
                          e.dma_start(out=o, in_=s), dma="gW%d" % pr)
                wo_pre.append(t)
    p.barrier()
    if stop_after == "s4":
        return _finish()

    ab2 = ABUF[:, :].rearrange("p (s h d) -> p s h d", s=2, h=32)
    tb2 = TBUF[:, 0:8192].rearrange("p (s c d) -> p s c d", s=2, c=32)
    at_sb = TBUF[:, 8192:12288].rearrange("p (c r) -> p c r", c=32)
    bcb = TBUF[:, 12288:16384]
    hT = STG[:, :, :].rearrange("p a b -> p (a b)")
    load_weight_cast(a_v, at_sb, 1, "gmisc")
    E("gpsimd", lambda e: e.dma_start(out=bcb, in_=bc_f), dma="gmisc")
    ta_misc = E("gpsimd", lambda e: e.dma_start(out=BIAS[:, :], in_=bias_f),
                dma="gmisc")
    hT_toks = [None] * NSLOT
    s6q = []
    s6state = {"slot_cons": [[], []], "pbuf1_cons": None,
               "txn2store": [None, None], "hT_prev": None}

    def s6_body(j, tsq_tok, ystore_tok):
        sl = j % 2
        sc6 = s6state
        ty = E("sync", lambda e, o=XBUF[:, sl, :], s=y_scr[j, :, :]:
               e.dma_start(out=o, in_=s),
               waits=sc6["slot_cons"][sl] + [ystore_tok], dma="lXB%d" % sl)
        sc6["slot_cons"][sl] = []
        tx = E("sync", lambda e, o=XBUF[:, 1 - sl, :], s=x_own[j, :, :]:
               e.dma_start(out=o, in_=s), waits=sc6["slot_cons"][1 - sl],
               dma="lXB%d" % (1 - sl))
        sc6["slot_cons"][1 - sl] = []
        L = LNS[:, sl, :]
        E("vector", lambda e, o=L[:, 0:1], i=LN2S[:, j, :]: e.tensor_reduce(
            out=o, in_=i, axis=AX.X, op=ALU.add), waits=[t_rstd[sl], tsq_tok])
        tcert = E("vector", lambda e, o=L[:, 2:3], i=LN2Q[:, j, :]: e.tensor_reduce(
            out=o, in_=i, axis=AX.X, op=ALU.add), inc=True, drain=True)
        mean, rstd, tr = ln_stats_fin(L[:, 0:1], L[:, 2:3], sl, [tcert])
        ty2 = E("vector", lambda e, o=XBUF[:, sl, :], m=mean, r=rstd: e.tensor_scalar(
            out=o, in0=o, scalar1=m, scalar2=r, op0=ALU.subtract, op1=ALU.mult),
          waits=[tr, ty], inc=True)
        t_lnuse[sl] = ty2
        p.D("vector")
        tx2 = E("vector", lambda e, o=XBUF[:, sl, :], b=XBUF[:, 1 - sl, :]:
                e.scalar_tensor_tensor(out=o, in0=o, scalar=c_one, in1=b,
                                       op0=ALU.mult, op1=ALU.add,
                                       accum_out=LNP[:, 2:3]),
                waits=[tx], inc=True, drain=True)
        sc6["slot_cons"][1 - sl].append(tx2)
        E("scalar", lambda e, o=PBUF[:, 0, :], i=XBUF[:, sl, :]:
          e.activation(out=o, in_=i, func=AF.Square, bias=0.0, scale=1.0,
                       accum_out=LNP[:, 3:4]), waits=[tx2])
        mean3, rstd3, tr3 = ln_stats_fin(LNP[:, 2:3], LNP[:, 3:4], sl, [tx2])
        txn2 = E("vector", lambda e, o=PBUF[:, 1, :], i=XBUF[:, sl, :], m=mean3,
                 r=rstd3: e.tensor_scalar(out=o, in0=i, scalar1=m, scalar2=r,
                                          op0=ALU.subtract, op1=ALU.mult),
                 waits=[tr3, sc6["pbuf1_cons"]], inc=True)
        t_lnuse[sl] = txn2
        sc6["slot_cons"][sl].append(txn2)
        tx2dma = E("scalar", lambda e, o=x2_scr[j, :, :], s=XBUF[:, sl, :]:
                   e.dma_start(out=o, in_=s), waits=[txn2], dma="sX2")
        sc6["slot_cons"][sl].append(tx2dma)
        tpe, tac = transpose_chunks(
            lambda c: PBUF[:, 1, c * 128:(c + 1) * 128],
            lambda c, sl=sl: tb2[:, sl, c, :],
            32, txn2, first_waits=[sc6["txn2store"][sl], sc6.get("hT_%d" % sl)])
        sc6["pbuf1_cons"] = tpe
        sc6["txn2store"][sl] = E("scalar", lambda e, o=xn2T_scr[j, :, :, :],
                                 s=tb2[:, sl, :, :]: e.dma_start(out=o, in_=s),
                                 waits=[tac], dma="sT2%d" % sl)
        bp2 = 2 + sl
        tmm = None
        for dc in range(32):
            w = ()
            if dc == 0:
                w = [tac, ta_misc, sc6["hT_prev"]]
            tmm = E("tensor", lambda e, o=fb[bp2][:, 0:128], l=at_sb[:, dc, :],
                    r=tb2[:, sl, dc, :], s=(dc == 0), t=(dc == 31):
                    e.matmul(o, l, r, start=s, stop=t), waits=w, inc=(dc == 31))
        th = E("vector", lambda e, o=hT[:, j * 128:(j + 1) * 128], i=fb[bp2][:, 0:128]:
               e.tensor_scalar(out=o, in0=i, scalar1=CO[:, 0:1], scalar2=None,
                               op0=ALU.mult), waits=[tmm], inc=True)
        hT_toks[j] = th
        sc6["hT_prev"] = th
        sc6["hT_%d" % sl] = tmm

    w_parts_done = [None, None]
    at_cons = [None, None]
    ev_done = [None, None]
    ystore = [None, None]
    for part in range(8):
        wsl = part % 2
        if part < 2:
            tw = wo_pre[part]
        else:
            tw = None
            for i in range(2):
                tw = E("gpsimd", lambda e, o=wb4[:, i * 16:(i + 1) * 16, wsl, :],
                       s=wo_v[:, i * 16:(i + 1) * 16, part * 512:(part + 1) * 512]:
                       e.dma_start(out=o, in_=s),
                       waits=[w_parts_done[wsl]] if i == 0 else (), dma="gW%d" % wsl)
        for j in range(NSLOT):
            sl = j % 2
            ta = E("sync", lambda e, o=ab2[:, sl, :, :], s=attn_scr[j, :, :, :]:
                   e.dma_start(out=o, in_=s), waits=[at_cons[sl]], dma="lAB%d" % sl)
            bp = j % 2
            tmm = None
            for dc in range(32):
                w = [ta, tw] + (ev_done[bp] or []) if dc == 0 else ()
                tmm = E("tensor", lambda e, o=fb[bp][:, :], l=ab2[:, sl, dc, :],
                        r=wb4[:, dc, wsl, :], s=(dc == 0), t=(dc == 31):
                        e.matmul(o, l, r, start=s, stop=t), waits=w, inc=(dc == 31))
            at_cons[sl] = tmm
            tev = E("vector", lambda e, o=YST[:, sl, :], i=fb[bp][:, :],
                    a=LN2S[:, j, part:part + 1]:
                    e.tensor_scalar(out=o, in0=i, scalar1=0.0, scalar2=None,
                                    op0=ALU.add, op1=ALU.add, accum_out=a),
                    waits=[tmm, ystore[sl]], inc=True)
            tsq = E("scalar", lambda e, o=PBUF[:, 0, 0:512], i=fb[bp][:, :],
                    a=LN2Q[:, j, part:part + 1]:
                    e.activation(out=o, in_=i, func=AF.Square, bias=0.0, scale=1.0,
                                 accum_out=a),
                    waits=[tev], inc=True)
            ev_done[bp] = [tev, tsq]
            ystore[sl] = E("scalar", lambda e, o=y_scr[j, :, part * 512:(part + 1) * 512],
                           s=YST[:, sl, :]: e.dma_start(out=o, in_=s),
                           waits=[tev], dma="sY%d" % sl)
            if part == 7:
                s6q.append((j, tsq, ystore[sl]))
                if j >= 2:
                    s6_body(*s6q[j - 2])
        w_parts_done[wsl] = (ENGSEM["tensor"], p.cnt["pe"])
    s6_body(*s6q[14])
    s6_body(*s6q[15])
    p.barrier()
    if stop_after in ("s5", "s6"):
        return _finish()

    xt2 = TBUF[:, 0:8192].rearrange("p (s c d) -> p s c d", s=2, c=32)

    s8q = []
    s8state = {"slot_cons": [[], []]}

    def s8_body(j, tsq_tok, fstore_tok):
        sl = j % 2
        sc8 = s8state
        tf = E("sync", lambda e, o=XBUF[:, sl, :], s=f_scr[j, :, :]:
               e.dma_start(out=o, in_=s), waits=sc8["slot_cons"][sl] + [fstore_tok],
               dma="lXB%d" % sl)
        sc8["slot_cons"][sl] = []
        tx2 = E("sync", lambda e, o=XBUF[:, 1 - sl, :], s=x2_scr[j, :, :]:
                e.dma_start(out=o, in_=s), waits=sc8["slot_cons"][1 - sl],
                dma="lXB%d" % (1 - sl))
        sc8["slot_cons"][1 - sl] = []
        L = LNS[:, sl, :]
        E("vector", lambda e, o=L[:, 0:1], i=LN4S[:, j, :]: e.tensor_reduce(
            out=o, in_=i, axis=AX.X, op=ALU.add), waits=[t_rstd[sl], tsq_tok])
        tcert = E("vector", lambda e, o=L[:, 2:3], i=LN4Q[:, j, :]: e.tensor_reduce(
            out=o, in_=i, axis=AX.X, op=ALU.add), inc=True, drain=True)
        mean, rstd, tr = ln_stats_fin(L[:, 0:1], L[:, 2:3], sl, [tcert])
        ty4 = E("vector", lambda e, o=XBUF[:, sl, :], m=mean, r=rstd: e.tensor_scalar(
            out=o, in0=o, scalar1=m, scalar2=r, op0=ALU.subtract, op1=ALU.mult),
          waits=[tr, tf], inc=True)
        t_lnuse[sl] = ty4
        p.D("vector")
        tout = E("vector", lambda e, o=XBUF[:, sl, :], b=XBUF[:, 1 - sl, :]:
                 e.tensor_tensor(out=o, in0=o, in1=b, op=ALU.add),
                 waits=[tx2], inc=True)
        sc8["slot_cons"][1 - sl].append(tout)
        todma = E("scalar", lambda e, o=out_own[j, :, :], s=XBUF[:, sl, :]:
                  e.dma_start(out=o, in_=s), waits=[tout], dma="sOUT")
        sc8["slot_cons"][sl].append(todma)

    xn2_cons = [None, None]
    w_parts_done = [None, None]
    ev_done = [None, None]
    fstore = [None, None]
    for part in range(8):
        wsl = part % 2
        tw = None
        for i in range(2):
            tw = E("gpsimd", lambda e, o=wb4[:, i * 16:(i + 1) * 16, wsl, :],
                   s=w_v[:, i * 16:(i + 1) * 16, part * 512:(part + 1) * 512]:
                   e.dma_start(out=o, in_=s),
                   waits=[w_parts_done[wsl]] if i == 0 else (), dma="gW%d" % wsl)
        for j in range(NSLOT):
            sl = j % 2
            txt = E("sync", lambda e, o=xt2[:, sl, :, :], s=xn2T_scr[j, :, :, :]:
                    e.dma_start(out=o, in_=s), waits=[xn2_cons[sl]], dma="lTB%d" % sl)
            bp = j % 2
            for dc in range(32):
                w = ()
                if dc == 0:
                    w = [txt, hT_toks[j], ta_misc, tw] + (ev_done[bp] or [])
                E("tensor", lambda e, o=fb[bp][:, :], l=xt2[:, sl, dc, :],
                  r=wb4[:, dc, wsl, :], s=(dc == 0):
                  e.matmul(o, l, r, start=s, stop=False), waits=w)
            E("tensor", lambda e, o=fb[bp][:, :], l=hT[:, j * 128:(j + 1) * 128],
              r=bcb[:, part * 512:(part + 1) * 512]:
              e.matmul(o, l, r, start=False, stop=False, skip_group_check=True))
            tmm = E("tensor", lambda e, o=fb[bp][:, :],
                    r=BIAS[:, part * 512:(part + 1) * 512]:
                    e.matmul(o, ORB[:, :], r, start=False, stop=True,
                             skip_group_check=True), inc=True)
            xn2_cons[sl] = tmm
            tev = E("vector", lambda e, o=YST[:, sl, :], i=fb[bp][:, :],
                    a=LN4S[:, j, part:part + 1]:
                    e.tensor_scalar(out=o, in0=i, scalar1=0.0, scalar2=None,
                                    op0=ALU.add, op1=ALU.add, accum_out=a),
                    waits=[tmm, fstore[sl]], inc=True)
            tsq = E("scalar", lambda e, o=PBUF[:, 0, 0:512], i=fb[bp][:, :],
                    a=LN4Q[:, j, part:part + 1]:
                    e.activation(out=o, in_=i, func=AF.Square, bias=0.0, scale=1.0,
                                 accum_out=a),
                    waits=[tev], inc=True)
            ev_done[bp] = [tev, tsq]
            fstore[sl] = E("scalar", lambda e, o=f_scr[j, :, part * 512:(part + 1) * 512],
                           s=YST[:, sl, :]: e.dma_start(out=o, in_=s),
                           waits=[tev], dma="sY%d" % sl)
            if part == 7:
                s8q.append((j, tsq, fstore[sl]))
                if j >= 1:
                    s8_body(*s8q[j - 1])
        w_parts_done[wsl] = (ENGSEM["tensor"], p.cnt["pe"])
    s8_body(*s8q[15])
    return _finish()


def _host_tables():
    pos = np.arange(S, dtype=np.float64)[:, None]
    freqs = np.exp(-np.arange(0, HD, 2, dtype=np.float64) * (math.log(10000.0) / HD))
    ang = pos * freqs
    c = np.cos(ang).T.astype(np.float32)
    s = np.sin(ang).T.astype(np.float32)
    return np.vstack([c, c]), np.vstack([s, s])


def make_in_maps(L0_attn_hidden_states, L0_attn_Wq, L0_attn_Wk, L0_attn_Wv, L0_attn_Wo,
                 L0_ffn_lora_A, L0_ffn_lora_B, L0_ffn_W, L0_ffn_bias,
                 L0_ffn_lora_weights):
    x = np.ascontiguousarray(np.asarray(L0_attn_hidden_states, dtype=np.float32))
    wq = np.ascontiguousarray(np.asarray(L0_attn_Wq, dtype=np.float32).T)
    wk = np.ascontiguousarray(np.asarray(L0_attn_Wk, dtype=np.float32).T)
    wv = np.ascontiguousarray(np.asarray(L0_attn_Wv, dtype=np.float32).T)
    wo = np.ascontiguousarray(np.asarray(L0_attn_Wo, dtype=np.float32).T)
    wff = np.ascontiguousarray(np.asarray(L0_ffn_W, dtype=np.float32).T)
    la = np.asarray(L0_ffn_lora_A, dtype=np.float32)
    lb = np.asarray(L0_ffn_lora_B, dtype=np.float32)
    a_t = np.ascontiguousarray(la.reshape(128, D).T)
    bc = np.ascontiguousarray(lb.transpose(0, 2, 1).reshape(128, D))
    bias = np.ascontiguousarray(np.asarray(L0_ffn_bias, dtype=np.float32).reshape(1, D))
    wrep = np.ascontiguousarray(np.repeat(
        np.asarray(L0_ffn_lora_weights, dtype=np.float32), 16)[:, None])

    cosk, sink = _host_tables()
    sc = 1.0 / math.sqrt(HD)

    ccol = np.zeros((128, 8), dtype=np.float32)
    ccol[:, 0] = 1.0
    ccol[:, 1] = -1.0
    ccol[:, 2] = 2.0
    ccol[:, 3] = 1e-5
    ccol[:, 4] = 1.0 / D
    ccol[:, 5] = -BOUND
    ident = np.eye(128, dtype=ml_dtypes.bfloat16)
    ones_col = np.ones((128, 1), dtype=ml_dtypes.bfloat16)
    ones_row_f = np.ones((1, 128), dtype=np.float32)
    ones_row_bf = np.ones((1, 128), dtype=ml_dtypes.bfloat16)
    ones_mat = np.ones((128, 128), dtype=ml_dtypes.bfloat16)

    kk = np.arange(128)[:, None]
    qq = np.arange(128)[None, :]
    diag = np.where(qq >= kk, 1e9, -1e9).astype(np.float32)
    allv = np.full((128, 128), 1e9, dtype=np.float32)
    allm = np.full((128, 128), -1e9, dtype=np.float32)

    def pair_mask(o):
        m = np.empty((128, 4, 2, 128), dtype=np.float32)
        for rel in range(4):
            for s in range(2):
                d = rel - 2 * s - o
                m[:, rel, s, :] = allv if d < 0 else (diag if d == 0 else allm)
        return np.ascontiguousarray(m.reshape(128, 4 * 256))

    in_maps = []
    for c in range(8):
        b, o = c % 4, c // 4
        tiles = (2 * np.arange(NSLOT) + o) * 128
        idx = (tiles[:, None] + np.arange(128)[None, :]).ravel()
        x_ownc = np.ascontiguousarray(x[b][idx].reshape(NSLOT, 128, D))
        cosq = np.ascontiguousarray(cosk[:, idx] * sc)
        sinq = np.ascontiguousarray(sink[:, idx] * sc)
        mask = pair_mask(o)
        in_maps.append({
            "x_own": x_ownc, "x_kv": x[b],
            "wq_t": wq, "wk_t": wk, "wv_t": wv, "wo_t": wo, "w_t": wff,
            "a_t": a_t, "bc_f": bc, "bias_f": bias, "wrep": wrep,
            "cosk_t": cosk, "sink_t": sink, "cosq_t": cosq, "sinq_t": sinq,
            "maskmin": np.ascontiguousarray(mask), "ccol": ccol,
            "ident": ident, "ones_col": ones_col, "ones_mat": ones_mat,
            "ones_row_f": ones_row_f, "ones_row_bf": ones_row_bf,
        })
    return in_maps


def kernel(**inputs):
    in_maps = make_in_maps(**inputs)
    nc = build_program()
    res = run_bass_kernel_spmd(nc, in_maps, core_ids=list(range(8)))

    out = np.empty((B, S, D), dtype=np.float32)
    for c in range(8):
        b, o = c % 4, c // 4
        oo = res.results[c]["out_own"]
        for j in range(NSLOT):
            t0 = (2 * j + o) * 128
            out[b, t0:t0 + 128, :] = oo[j]
    return out
